# version 09
# baseline (speedup 1.0000x reference)
import numpy as np
import concourse.bass as bass
import concourse.mybir as mybir
from concourse.bass_core_rust import IndirectOffsetOnAxis
from concourse.bass_utils import run_bass_kernel_spmd

F32 = mybir.dt.float32
BF16 = mybir.dt.bfloat16
AF = mybir.ActivationFunctionType
ALU = mybir.AluOpType
AX = mybir.AxisListType

N, C = 2048, 1024
NT = 16
CT = 8
E, I = 8, 2816
IT = 22
NH = 2
HN = N // NH
HT = NT // NH
CAP = 768
CAPP = 1024
ST = CAP // 128
TCS = [(0, 512), (512, CAP - 512)]
SCALE = float(C) ** -0.5
EPS = 1e-5

ENGS = ("sync", "tensor", "vector", "scalar", "gpsimd")


class R:
    __slots__ = ("w", "readers", "init_deps", "name")

    def __init__(self, name="r"):
        self.w = None
        self.readers = []
        self.init_deps = []
        self.name = name


class Prog:
    def __init__(self, nc):
        self.nc = nc
        self.ops = {e: [] for e in ENGS}
        self.sems = {e: nc.alloc_semaphore("sem_" + e) for e in ENGS}
        self.counts = {e: 0 for e in ENGS}
        self.last_wait = {e: {} for e in ENGS}

    def _op(self, eng, emit, toks, sig=True, dma=False):
        lw = self.last_wait[eng]
        waits = []
        for (s, v) in sorted(set(toks)):
            if s == eng and v > self.counts[eng]:
                raise AssertionError(f"future self-wait {s} {v}")
            if lw.get(s, 0) < v:
                waits.append((s, v))
                lw[s] = v
        inc = 16 if dma else (1 if sig else 0)
        tok = None
        if inc:
            self.counts[eng] += inc
            tok = (eng, self.counts[eng])
        self.ops[eng].append((waits, emit, self.sems[eng] if inc else None, inc))
        return tok

    def _gather(self, reads, writes, extra):
        toks = [t for t in extra if t is not None]
        for r in reads:
            if r.w is not None:
                toks.append(r.w)
            toks.extend(r.init_deps)
            r.init_deps = []
        for r in writes:
            if r.w is not None:
                toks.append(r.w)
            toks.extend(t for t in r.readers if t is not None)
            toks.extend(r.init_deps)
            r.init_deps = []
        return toks

    def do(self, eng, emit, reads=(), writes=(), extra=(), dma=False):
        toks = self._gather(reads, writes, extra)
        tok = self._op(eng, emit, toks, sig=True, dma=dma)
        for r in reads:
            r.readers.append(tok)
        for r in writes:
            r.w = tok
            r.readers = []
        return tok

    def mm_group(self, emits, reads=(), writes=(), extra=(), per_reads=None,
                 reg_reads=()):
        toks = self._gather(reads, writes, extra)
        n = len(emits)
        tok = None
        allr = list(reads)
        for i, e in enumerate(emits):
            ti = list(toks) if i == 0 else []
            if per_reads is not None:
                for r in per_reads[i]:
                    if r.w is not None:
                        ti.append(r.w)
                    ti.extend(r.init_deps)
                    r.init_deps = []
                    allr.append(r)
            t = self._op("tensor", e, ti, sig=(i == n - 1))
            if t is not None:
                tok = t
        allr.extend(reg_reads)
        for r in allr:
            r.readers.append(tok)
        for r in writes:
            r.w = tok
            r.readers = []
        return tok

    def emit(self):
        nc = self.nc
        with nc.Block() as block:
            for eng_name in ENGS:
                sems = self.sems

                def make(oplist):
                    def run(eng):
                        for waits, emit, inc_sem, inc_amt in oplist:
                            for (s, v) in waits:
                                eng.wait_ge(sems[s], v)
                            inst = emit(eng)
                            if inc_sem is not None:
                                inst.then_inc(inc_sem, inc_amt)
                    return run

                getattr(block, eng_name)(make(self.ops[eng_name]))


def build(upto="full", n_experts=E):
    nc = bass.Bass("TRN2", target_bir_lowering=False)

    x_d = nc.dram_tensor("x", [N, C], F32, kind="ExternalInput").ap()
    wqkv_d = nc.dram_tensor("wqkv", [3 * C, C], F32, kind="ExternalInput").ap()
    wconv_d = nc.dram_tensor("wconv", [1, C], F32, kind="ExternalInput").ap()
    router_d = nc.dram_tensor("router", [E, C], F32, kind="ExternalInput").ap()
    gate_d = nc.dram_tensor("gate", [E * I, C], F32, kind="ExternalInput").ap()
    up_d = nc.dram_tensor("up", [E * I, C], F32, kind="ExternalInput").ap()
    down_d = nc.dram_tensor("down", [E * C, I], F32, kind="ExternalInput").ap()
    out_d = nc.dram_tensor("out", [N, C], F32, kind="ExternalOutput").ap()

    gate_scr = nc.dram_tensor("gate_scr", [E * I, C], BF16, kind="Internal").ap()
    up_scr = nc.dram_tensor("up_scr", [E * I, C], BF16, kind="Internal").ap()
    down_scr = nc.dram_tensor("down_scr", [E * C, I], BF16, kind="Internal").ap()
    y1b_scr = nc.dram_tensor("y1b_scr", [N, C], BF16, kind="Internal").ap()
    y1f_scr = nc.dram_tensor("y1f_scr", [N, C], F32, kind="Internal").ap()
    ybuf_scr = nc.dram_tensor("ybuf_scr", [E * CAPP, C], BF16, kind="Internal").ap()
    ye_scr = nc.dram_tensor("ye_scr", [E * CAPP, C], F32, kind="Internal").ap()
    ytok_scr = nc.dram_tensor("ytok_scr", [N, C], F32, kind="Internal").ap()

    ident_f = nc.alloc_sbuf_tensor("ident_f", [128, 128], F32)
    ones_f = nc.alloc_sbuf_tensor("ones_f", [1, 128], F32)
    wrow = nc.alloc_sbuf_tensor("wrow", [1, 128], F32)
    bc_sb = nc.alloc_sbuf_tensor("bc_sb", [128, 1024], F32)
    sqd = nc.alloc_sbuf_tensor("sqd", [128, 1024], F32)
    sumc = nc.alloc_sbuf_tensor("sumc", [128, NT], F32)
    sqc = nc.alloc_sbuf_tensor("sqc", [128, NT], F32)
    meanc = nc.alloc_sbuf_tensor("meanc", [128, NT], F32)
    rstdc = nc.alloc_sbuf_tensor("rstdc", [128, NT], F32)
    tmpc = nc.alloc_sbuf_tensor("tmpc", [128, NT], F32)
    varc = nc.alloc_sbuf_tensor("varc", [128, NT], F32)
    stdc = nc.alloc_sbuf_tensor("stdc", [128, NT], F32)
    agc = nc.alloc_sbuf_tensor("agc", [128, NT], F32)
    s2c = nc.alloc_sbuf_tensor("s2c", [128, NT], F32)
    w2c = nc.alloc_sbuf_tensor("w2c", [128, NT], F32)
    mxc = nc.alloc_sbuf_tensor("mxc", [128, 4], F32)
    epsc = nc.alloc_sbuf_tensor("epsc", [128, 1], F32)
    cols_bf = nc.alloc_sbuf_tensor("cols_bf", [128, 8], BF16)
    W0 = nc.alloc_sbuf_tensor("W0", [128, IT * 1024], BF16)
    W1 = nc.alloc_sbuf_tensor("W1", [128, IT * 1024], BF16)
    RwT = nc.alloc_sbuf_tensor("RwT", [128, CT * E], BF16)

    rem = nc.sbuf_bytes_remaining
    top = 229344
    base = top - rem

    def at(name, shape, dtype, off):
        return nc.alloc_sbuf_tensor_at(name, shape, dtype, offset=base + off)

    x_tm = at("x_tm", [128, NT * 1024], F32, 0)
    qkv_buf = at("qkv_buf", [128, CT * 1024], BF16, 65536)
    y1b_st = at("y1b_st", [128, 2 * 1024], BF16, 65536 + 16384)
    rows = at("rows", [1, 2048], F32, 65536 + 16384 + 4096)
    wstg = at("wstg", [128, 2048], F32, 65536 + 16384 + 4096 + 8192)
    attn_end = 65536 + 16384 + 4096 + 8192 + 8192
    Hg = at("Hg", [128, IT * CAP], BF16, 0)
    o = IT * CAP * 2
    y1g = at("y1g", [128, 2 * CT * CAP], BF16, o); o += 2 * CT * CAP * 2
    Ye_sb = at("Ye_sb", [128, CT * CAP], F32, o); o += CT * CAP * 4
    Hu = at("Hu", [128, 2 * CAP], BF16, o); o += 2 * CAP * 2
    ln2t = at("ln2t", [128, 2048], F32, o); o += 8192
    y1r = at("y1r", [128, 1024], F32, o); o += 4096
    yetm = at("yetm", [128, 2048], F32, o)
    ysc = at("ysc", [128, 2048], BF16, o)
    o += 8192
    zero_t = at("zero_t", [128, 1024], BF16, o); o += 2048
    probs = at("probs", [128, HT * E], F32, o); o += 256
    wte = at("wte", [128, NT * E], F32, o); o += 512
    ge1 = at("ge1", [128, 2 * E], F32, o); o += 64
    addri = at("addri", [128, NT * E], mybir.dt.int32, o); o += 512
    U128 = at("U128", [128, 128], F32, o); o += 512
    maskt = at("maskt", [128, NT * E], F32, o); o += 512
    postile = at("postile", [128, E], F32, o); o += 64
    offsall = at("offsall", [1, 128], F32, o); o += 512
    onescol = at("onescol", [128, 1], F32, o); o += 32
    tmpa = at("tmpa", [128, E], F32, o); o += 64
    tmpb = at("tmpb", [128, E], F32, o); o += 64
    rmx = at("rmx", [128, 2 * NT], F32, o); o += 128
    ffn_end = o
    assert base + max(ffn_end, attn_end) <= top - 4, (base, ffn_end)

    psT = [nc.alloc_psum_tensor(f"ps{i}", [128, 1024], F32) for i in range(4)]

    P = Prog(nc)

    slot_t = [(psT[i // 2], (i % 2) * 512) for i in range(8)]
    slotR = [R(f"ps{i}") for i in range(8)]

    def slot_ap(i, p0, pn, f0, fn):
        t, off = slot_t[i]
        return t[p0:p0 + pn, off + f0:off + fn]

    def pair_ap(pi, f0, fn):
        return psT[pi][:, f0:fn]

    def pairR(pi):
        return (slotR[2 * pi], slotR[2 * pi + 1])

    xt = [R(f"x{j}") for j in range(NT)]
    r_qkv = [R(f"qkv{c}") for c in range(CT)]
    r_rows = [R("rows0"), R("rows1")]
    r_wstg = [R("wstg0"), R("wstg1")]
    r_wrow = R("wrow")
    r_bc = R("bc")
    r_sqd = R("sqd")
    r_agc, r_s2c, r_w2c = R("agc"), R("s2c"), R("w2c")
    r_mxc = R("mxc")
    r_colsbf = R("cols_bf")
    r_sumc = [R(f"sumc{j}") for j in range(NT)]
    r_sqc = [R(f"sqc{j}") for j in range(NT)]
    r_meanc = [R(f"meanc{j}") for j in range(NT)]
    r_rstdc = [R(f"rstdc{j}") for j in range(NT)]
    r_tmpc = [R(f"tmpc{j}") for j in range(NT)]
    r_varc = [R(f"varc{j}") for j in range(NT)]
    r_stdc = [R(f"stdc{j}") for j in range(NT)]
    r_y1bst = [R("y1b_st0"), R("y1b_st1")]
    r_idf, r_ones = R("ident_f"), R("ones_f")
    r_eps = R("epsc")
    r_W = [[R(f"W0_{i}") for i in range(IT)], [R(f"W1_{i}") for i in range(IT)]]
    r_RwT = R("RwT")
    r_gscr = [R(f"gscr{e}") for e in range(E)]
    r_uscr = [R(f"uscr{e}") for e in range(E)]
    r_dscr = [R(f"dscr{e}") for e in range(E)]
    r_y1bscr = [R(f"y1bscr{j}") for j in range(NT)]
    r_y1fscr = [R(f"y1fscr{j}") for j in range(NT)]
    r_Hg = [R(f"Hg{i}") for i in range(IT)]
    r_rtg = [R(f"rtg{c}") for c in range(CT)]
    r_y1g = [[R(f"y1g{b}_{c}") for c in range(CT)] for b in range(2)]
    r_rmx = [R(f"rmx{j}") for j in range(NT)]
    r_Yesb = [[R(f"Ye{h}_{t}") for t in range(len(TCS))] for h in range(CT)]
    r_Hu = [R("Hu0"), R("Hu1")]
    r_ln2t = [R("ln2t0"), R("ln2t1")]
    r_y1r = R("y1r")
    r_yetm = [R("yetm0"), R("yetm1")]
    r_ysc = [R("ysc0"), R("ysc1")]
    r_zero = R("zero_t")
    r_probs = [R(f"probs{j}") for j in range(HT)]
    r_wte = [R(f"wte{j}") for j in range(NT)]
    r_ge1 = [R("ge1a"), R("ge1b")]
    r_maskt = [R(f"maskt{j}") for j in range(NT)]
    r_addr = [R(f"addr{j}") for j in range(NT)]
    r_U, r_onescol, r_offsall = R("U128"), R("onescol"), R("offsall")
    r_postile, r_tmpa, r_tmpb = R("postile"), R("tmpa"), R("tmpb")
    r_Ytok = [R(f"Ytok{j}") for j in range(NT)]
    r_Yescr = [R(f"Yescr{e}") for e in range(E)]

    P.do("gpsimd", lambda g: g.memset(ident_f[:], 0.0), writes=[r_idf])
    P.do("gpsimd", lambda g: g.affine_select(
        out=ident_f[:], in_=ident_f[:], compare_op=ALU.not_equal, fill=1.0,
        base=0, pattern=[[-1, 128]], channel_multiplier=1),
        writes=[r_idf])
    P.do("vector", lambda v: v.memset(ones_f[:], 1.0), writes=[r_ones])
    P.do("vector", lambda v: v.memset(epsc[:], EPS), writes=[r_eps])

    def do_cast(e, extra=()):
        P.do("gpsimd", lambda g, e=e: g.dma_start(
            out=gate_scr[e * I:(e + 1) * I, :], in_=gate_d[e * I:(e + 1) * I, :]),
            writes=[r_gscr[e]], extra=extra, dma=True)
        P.do("gpsimd", lambda g, e=e: g.dma_start(
            out=up_scr[e * I:(e + 1) * I, :], in_=up_d[e * I:(e + 1) * I, :]),
            writes=[r_uscr[e]], extra=extra, dma=True)
        P.do("gpsimd", lambda g, e=e: g.dma_start(
            out=down_scr[e * C:(e + 1) * C, :], in_=down_d[e * C:(e + 1) * C, :]),
            writes=[r_dscr[e]], extra=extra, dma=True)

    def load_gateup(e, W, rW, scr, rscr):
        for cc in range(CT):
            P.do("sync", lambda s, cc=cc, e=e, W=W: s.dma_start(
                out=W[:, cc * I:(cc + 1) * I],
                in_=scr[e * I:(e + 1) * I, cc * 128:(cc + 1) * 128],
                transpose=True),
                reads=[rscr], writes=[rW[cc]], dma=True)

    tok_x = P.do("scalar", lambda s: s.dma_start(
        out=x_tm[:].rearrange("p (j c) -> p j c", c=1024),
        in_=x_d.rearrange("(j p) c -> p j c", p=128)), dma=True)
    for j in range(NT):
        xt[j].w = tok_x
    P.do("scalar", lambda s: s.dma_start(out=rows[0:1, 0:1024], in_=wconv_d),
         writes=[r_rows[0]], dma=True)

    SL = [0]

    def next_slot():
        i = SL[0] % 8
        SL[0] += 1
        return i

    def next_pair():
        i0 = (SL[0] % 4) * 2
        SL[0] += 2
        return i0

    def row_to_cols_bf(row_slot):
        for c in range(CT):
            si = next_slot()
            P.mm_group([lambda t, c=c, si=si, rs=row_slot: t.matmul(
                slot_ap(si, 0, 128, 0, 1),
                rows[0:1, rs * 1024 + c * 128:rs * 1024 + (c + 1) * 128],
                ones_f[0:1, 0:1], start=True, stop=True)],
                reads=[r_rows[row_slot], r_ones], writes=[slotR[si]])
            P.do("scalar", lambda s, c=c, si=si: s.copy(
                out=cols_bf[:, c:c + 1], in_=slot_ap(si, 0, 128, 0, 1)),
                reads=[slotR[si]], writes=[r_colsbf])

    def broadcast_row_to_bc(row_slot):
        i0 = next_pair()
        for h in range(2):
            P.mm_group([lambda t, h=h, i0=i0, rs=row_slot: t.matmul(
                slot_ap(i0 + h, 0, 128, 0, 512),
                ones_f[0:1, 0:128],
                rows[0:1, rs * 1024 + h * 512:rs * 1024 + (h + 1) * 512],
                start=True, stop=True)],
                reads=[r_rows[row_slot], r_ones], writes=[slotR[i0 + h]])
        P.do("vector", lambda v, i0=i0: v.tensor_copy(
            out=bc_sb[:], in_=pair_ap(i0 // 2, 0, 1024)),
            reads=[slotR[i0], slotR[i0 + 1]], writes=[r_bc])

    def matvec_row(lhs_cols_res, lhs_col_ap, rhs_buf_ap, rhs_res, out_row_slot,
                   nk=CT, post_scale=None):
        i0 = next_pair()
        for h in range(2):
            emits = [lambda t, k=k, h=h, i0=i0: t.matmul(
                slot_ap(i0 + h, 0, 1, 0, 512),
                lhs_col_ap(k),
                rhs_buf_ap(k, h),
                start=(k == 0), stop=(k == nk - 1)) for k in range(nk)]
            P.mm_group(emits, reads=list(lhs_cols_res) + list(rhs_res),
                       writes=[slotR[i0 + h]])
        if post_scale is None:
            P.do("vector", lambda v, i0=i0, rs=out_row_slot: v.tensor_copy(
                out=rows[0:1, rs * 1024:(rs + 1) * 1024], in_=pair_ap(i0 // 2, 0, 1024)[0:1, :]),
                reads=[slotR[i0], slotR[i0 + 1]], writes=[r_rows[out_row_slot]])
        else:
            P.do("vector", lambda v, i0=i0, rs=out_row_slot, sc=post_scale: v.tensor_scalar(
                out=rows[0:1, rs * 1024:(rs + 1) * 1024], in0=pair_ap(i0 // 2, 0, 1024)[0:1, :],
                scalar1=sc, scalar2=None, op0=ALU.mult),
                reads=[slotR[i0], slotR[i0 + 1]], writes=[r_rows[out_row_slot]])

    def load_nat_bf(row0):
        for r in range(CT):
            pp = r % 2
            P.do("scalar", lambda s, r=r, pp=pp: s.dma_start(
                out=wstg[:, pp * 1024:(pp + 1) * 1024],
                in_=wqkv_d[row0 + r * 128:row0 + (r + 1) * 128, :]),
                writes=[r_wstg[pp]], dma=True)
            P.do("scalar", lambda s, r=r, pp=pp: s.copy(
                out=qkv_buf[:, r * 1024:(r + 1) * 1024],
                in_=wstg[:, pp * 1024:(pp + 1) * 1024]),
                reads=[r_wstg[pp]], writes=[r_qkv[r]])

    def load_T_bf(row0):
        for r in range(CT):
            pp = r % 2
            P.do("scalar", lambda s, r=r, pp=pp: s.dma_start(
                out=wstg[:, pp * 1024:(pp + 1) * 1024],
                in_=wqkv_d[row0 + r * 128:row0 + (r + 1) * 128, :]),
                writes=[r_wstg[pp]], dma=True)
            for t in range(CT):
                si = next_slot()
                P.mm_group([lambda tn, r=r, t=t, pp=pp, si=si: tn.transpose(
                    slot_ap(si, 0, 128, 0, 128),
                    wstg[:, pp * 1024 + t * 128:pp * 1024 + (t + 1) * 128],
                    ident_f[:])],
                    reads=[r_wstg[pp], r_idf], writes=[slotR[si]])
                P.do("scalar", lambda s, r=r, t=t, si=si: s.copy(
                    out=qkv_buf[:, t * 1024 + r * 128:t * 1024 + (r + 1) * 128],
                    in_=slot_ap(si, 0, 128, 0, 128)),
                    reads=[slotR[si]], writes=[r_qkv[t]])

    load_nat_bf(0)
    tok_wq = ("scalar", P.counts["scalar"])
    do_cast(0, extra=[tok_wq])
    do_cast(1, extra=[tok_wq])
    load_gateup(0, W0, r_W[0], gate_scr, r_gscr[0])
    load_gateup(0, W1, r_W[1], up_scr, r_uscr[0])
    row_to_cols_bf(0)
    matvec_row([r_colsbf], lambda k: cols_bf[:, k:k + 1],
               lambda k, h: qkv_buf[:, k * 1024 + h * 512:k * 1024 + (h + 1) * 512],
               r_qkv, 1)
    broadcast_row_to_bc(1)
    for j in range(NT):
        P.do("vector", lambda v, j=j: v.scalar_tensor_tensor(
            out=sqd[:], in0=x_tm[:, j * 1024:(j + 1) * 1024], scalar=1.0,
            in1=bc_sb[:], op0=ALU.mult, op1=ALU.mult,
            accum_out=agc[:, j:j + 1]),
            reads=[xt[j], r_bc], writes=[r_sqd, r_agc])
    matvec_row([r_agc] + xt, lambda k: agc[:, k:k + 1],
               lambda k, h: x_tm[:, k * 1024 + h * 512:k * 1024 + (h + 1) * 512],
               [], 0, nk=NT, post_scale=SCALE)
    row_to_cols_bf(0)
    load_T_bf(C)
    matvec_row([r_colsbf], lambda k: cols_bf[:, k:k + 1],
               lambda k, h: qkv_buf[:, k * 1024 + h * 512:k * 1024 + (h + 1) * 512],
               r_qkv, 0)
    P.do("vector", lambda v: v.reduce_max(out=mxc[0:1, 0:1], in_=rows[0:1, 0:1024],
                                          axis=AX.X, negate=True),
         reads=[r_rows[0]], writes=[r_mxc])
    P.do("scalar", lambda s: s.activation(out=rows[0:1, 0:1024], in_=rows[0:1, 0:1024],
                                          func=AF.Exp, bias=mxc[0:1, 0:1]),
         reads=[r_mxc], writes=[r_rows[0]])
    P.do("vector", lambda v: v.reduce_sum(out=mxc[0:1, 1:2], in_=rows[0:1, 0:1024], axis=AX.X),
         reads=[r_rows[0]], writes=[r_mxc])
    P.do("vector", lambda v: v.reciprocal(out=mxc[0:1, 2:3], in_=mxc[0:1, 1:2]),
         reads=[r_mxc], writes=[r_mxc])
    P.do("vector", lambda v: v.tensor_scalar(out=rows[0:1, 0:1024], in0=rows[0:1, 0:1024],
                                             scalar1=mxc[0:1, 2:3], scalar2=None, op0=ALU.mult),
         reads=[r_mxc], writes=[r_rows[0]])
    row_to_cols_bf(0)
    load_nat_bf(2 * C)
    matvec_row([r_colsbf], lambda k: cols_bf[:, k:k + 1],
               lambda k, h: qkv_buf[:, k * 1024 + h * 512:k * 1024 + (h + 1) * 512],
               r_qkv, 1)
    broadcast_row_to_bc(1)
    for j in range(NT):
        P.do("vector", lambda v, j=j: v.scalar_tensor_tensor(
            out=sqd[:], in0=x_tm[:, j * 1024:(j + 1) * 1024], scalar=1.0,
            in1=bc_sb[:], op0=ALU.mult, op1=ALU.mult,
            accum_out=s2c[:, j:j + 1]),
            reads=[xt[j], r_bc], writes=[r_sqd, r_s2c])
    P.do("vector", lambda v: v.reduce_max(out=mxc[:, 0:1], in_=s2c[:], axis=AX.X),
         reads=[r_s2c], writes=[r_mxc])
    si = next_slot()
    P.mm_group([lambda t, si=si: t.matmul(
        slot_ap(si, 0, 1, 0, 128), mxc[:, 0:1], ident_f[:], start=True, stop=True)],
        reads=[r_mxc, r_idf], writes=[slotR[si]])
    P.do("vector", lambda v, si=si: v.tensor_copy(out=wrow[:], in_=slot_ap(si, 0, 1, 0, 128)),
         reads=[slotR[si]], writes=[r_wrow])
    P.do("vector", lambda v: v.reduce_max(out=mxc[0:1, 1:2], in_=wrow[:],
                                          axis=AX.X, negate=True),
         reads=[r_wrow], writes=[r_mxc])
    si = next_slot()
    P.mm_group([lambda t, si=si: t.matmul(
        slot_ap(si, 0, 128, 0, 1), ones_f[0:1, 0:128], mxc[0:1, 1:2], start=True, stop=True)],
        reads=[r_mxc, r_ones], writes=[slotR[si]])
    P.do("vector", lambda v, si=si: v.tensor_copy(out=mxc[:, 2:3], in_=slot_ap(si, 0, 128, 0, 1)),
         reads=[slotR[si]], writes=[r_mxc])
    P.do("scalar", lambda s: s.activation(out=w2c[:], in_=s2c[:], func=AF.Exp,
                                          bias=mxc[:, 2:3]),
         reads=[r_s2c, r_mxc], writes=[r_w2c])
    P.do("vector", lambda v: v.reduce_sum(out=mxc[:, 0:1], in_=w2c[:], axis=AX.X),
         reads=[r_w2c], writes=[r_mxc])
    si = next_slot()
    P.mm_group([lambda t, si=si: t.matmul(
        slot_ap(si, 0, 1, 0, 128), mxc[:, 0:1], ident_f[:], start=True, stop=True)],
        reads=[r_mxc, r_idf], writes=[slotR[si]])
    P.do("vector", lambda v, si=si: v.tensor_copy(out=wrow[:], in_=slot_ap(si, 0, 1, 0, 128)),
         reads=[slotR[si]], writes=[r_wrow])
    P.do("vector", lambda v: v.reduce_sum(out=mxc[0:1, 1:2], in_=wrow[:], axis=AX.X),
         reads=[r_wrow], writes=[r_mxc])
    P.do("vector", lambda v: v.reciprocal(out=mxc[0:1, 1:2], in_=mxc[0:1, 1:2]),
         reads=[r_mxc], writes=[r_mxc])
    si = next_slot()
    P.mm_group([lambda t, si=si: t.matmul(
        slot_ap(si, 0, 128, 0, 1), ones_f[0:1, 0:128], mxc[0:1, 1:2], start=True, stop=True)],
        reads=[r_mxc, r_ones], writes=[slotR[si]])
    P.do("vector", lambda v, si=si: v.tensor_copy(out=mxc[:, 3:4], in_=slot_ap(si, 0, 128, 0, 1)),
         reads=[slotR[si]], writes=[r_mxc])
    P.do("vector", lambda v: v.tensor_scalar(out=w2c[:], in0=w2c[:], scalar1=mxc[:, 3:4],
                                             scalar2=None, op0=ALU.mult),
         reads=[r_mxc], writes=[r_w2c])
    matvec_row([r_w2c] + xt, lambda k: w2c[:, k:k + 1],
               lambda k, h: x_tm[:, k * 1024 + h * 512:k * 1024 + (h + 1) * 512],
               [], 0, nk=NT)
    row_to_cols_bf(0)
    load_T_bf(2 * C)
    matvec_row([r_colsbf], lambda k: cols_bf[:, k:k + 1],
               lambda k, h: qkv_buf[:, k * 1024 + h * 512:k * 1024 + (h + 1) * 512],
               r_qkv, 1)
    broadcast_row_to_bc(1)

    for j in range(NT):
        xa = x_tm[:, j * 1024:(j + 1) * 1024]
        P.do("vector", lambda v, xa=xa, j=j: v.scalar_tensor_tensor(
            out=xa, in0=xa, scalar=1.0, in1=bc_sb[:], op0=ALU.mult, op1=ALU.add,
            accum_out=sumc[:, j:j + 1]),
            reads=[r_bc], writes=[xt[j], r_sumc[j]])
        P.do("scalar", lambda s, xa=xa, j=j: s.activation(
            out=sqd[:], in_=xa, func=AF.Square, accum_out=sqc[:, j:j + 1]),
            reads=[xt[j]], writes=[r_sqd, r_sqc[j]])
        P.do("vector", lambda v, j=j: v.tensor_scalar(
            out=meanc[:, j:j + 1], in0=sumc[:, j:j + 1], scalar1=1.0 / C, scalar2=None,
            op0=ALU.mult), reads=[r_sumc[j]], writes=[r_meanc[j]])
        P.do("vector", lambda v, j=j: v.tensor_tensor(
            out=tmpc[:, j:j + 1], in0=meanc[:, j:j + 1], in1=meanc[:, j:j + 1],
            op=ALU.mult), reads=[r_meanc[j]], writes=[r_tmpc[j]])
        P.do("vector", lambda v, j=j: v.scalar_tensor_tensor(
            out=varc[:, j:j + 1], in0=sqc[:, j:j + 1], scalar=1.0 / C,
            in1=tmpc[:, j:j + 1], op0=ALU.mult, op1=ALU.subtract),
            reads=[r_sqc[j], r_tmpc[j]], writes=[r_varc[j]])
        P.do("scalar", lambda s, j=j: s.activation(
            out=stdc[:, j:j + 1], in_=varc[:, j:j + 1], func=AF.Sqrt, bias=epsc[:]),
            reads=[r_varc[j], r_eps], writes=[r_stdc[j]])
        P.do("vector", lambda v, j=j: v.reciprocal(
            out=rstdc[:, j:j + 1], in_=stdc[:, j:j + 1]),
            reads=[r_stdc[j]], writes=[r_rstdc[j]])
        P.do("vector", lambda v, xa=xa, j=j: v.tensor_scalar(
            out=xa, in0=xa, scalar1=meanc[:, j:j + 1], scalar2=rstdc[:, j:j + 1],
            op0=ALU.subtract, op1=ALU.mult),
            reads=[r_meanc[j], r_rstdc[j]], writes=[xt[j]])
        st = j % 2
        P.do("scalar", lambda s, xa=xa, st=st: s.copy(
            out=y1b_st[:, st * 1024:(st + 1) * 1024], in_=xa),
            reads=[xt[j]], writes=[r_y1bst[st]])
        P.do("scalar", lambda s, j=j, st=st: s.dma_start(
            out=y1b_scr[j * 128:(j + 1) * 128, :],
            in_=y1b_st[:, st * 1024:(st + 1) * 1024]),
            reads=[r_y1bst[st]], writes=[r_y1bscr[j]], dma=True)
        P.do("scalar", lambda s, xa=xa, j=j: s.dma_start(
            out=y1f_scr[j * 128:(j + 1) * 128, :], in_=xa),
            reads=[xt[j]], writes=[r_y1fscr[j]], dma=True)

    if upto == "y1":
        for j in range(NT):
            P.do("scalar", lambda s, j=j: s.dma_start(
                out=out_d[j * 128:(j + 1) * 128, :],
                in_=x_tm[:, j * 1024:(j + 1) * 1024]),
                reads=[xt[j]], dma=True)
        final = P.counts["scalar"]
        P._op("scalar", lambda s: s.nop(), [("scalar", final)], sig=False)
        P.emit()
        return nc

    attn_done = []
    for rr in xt + r_qkv + [r_rows[0], r_rows[1], r_y1bst[0], r_y1bst[1]]:
        if rr.w is not None:
            attn_done.append(rr.w)
        attn_done.extend(t for t in rr.readers if t is not None)
    attn_done = sorted(set(attn_done))
    overlay_rs = (r_ln2t + [r_y1r, r_Hu[0], r_Hu[1], r_ge1[0], r_ge1[1], r_zero, r_U,
                            r_onescol, r_offsall, r_postile, r_tmpa, r_tmpb]
                  + r_Hg + r_rtg + r_y1g[0] + r_y1g[1] + r_rmx
                  + r_probs + r_wte + r_maskt + r_addr
                  + r_yetm + r_ysc
                  + [r_Yesb[h][t] for h in range(CT) for t in range(len(TCS))])
    for rr in overlay_rs:
        rr.init_deps = list(attn_done)

    P.do("vector", lambda v: v.memset(zero_t[:], 0.0), writes=[r_zero])
    P.do("vector", lambda v: v.memset(onescol[:], 1.0), writes=[r_onescol])
    P.do("vector", lambda v: v.memset(offsall[:], 0.0), writes=[r_offsall])
    P.do("gpsimd", lambda g: g.memset(U128[:], 0.0), writes=[r_U])
    P.do("gpsimd", lambda g: g.affine_select(
        out=U128[:], in_=U128[:], compare_op=ALU.is_gt, fill=1.0,
        base=0, pattern=[[-1, 128]], channel_multiplier=1), writes=[r_U])

    ybuf_zero_tok = [[] for _ in range(E)]
    for e in range(E):
        for k in range(ST):
            t = P.do("sync", lambda s, e=e, k=k: s.dma_start(
                out=ybuf_scr[e * CAPP + k * 128:e * CAPP + (k + 1) * 128, :],
                in_=zero_t[:]), reads=[r_zero], dma=True)
            ybuf_zero_tok[e].append(t)
        P.do("gpsimd", lambda g, e=e: g.dma_start(
            out=ye_scr[e * CAPP + CAP:e * CAPP + CAP + 1, :],
            in_=zero_t[0:1, :]), reads=[r_zero], writes=[r_Yescr[e]], dma=True)

    P.do("scalar", lambda s: s.dma_start(out=bc_sb[0:E, :], in_=router_d),
         writes=[r_bc], dma=True)
    for cc in range(CT):
        si = next_slot()
        P.mm_group([lambda t, cc=cc, si=si: t.transpose(
            slot_ap(si, 0, 128, 0, 8), bc_sb[0:E, cc * 128:(cc + 1) * 128],
            ident_f[0:E, 0:E])],
            reads=[r_bc, r_idf], writes=[slotR[si]])
        P.do("scalar", lambda s, cc=cc, si=si: s.copy(
            out=RwT[:, cc * E:(cc + 1) * E], in_=slot_ap(si, 0, 128, 0, 8)),
            reads=[slotR[si]], writes=[r_RwT])

    rtg = Hg
    ybuf_ready = [[] for _ in range(E)]
    for half in range(NH):
        t0 = half * HN
        y1src = [r_y1bscr[half * HT + jj] for jj in range(HT)]
        for cc in range(CT):
            P.do("scalar", lambda s, cc=cc, t0=t0: s.dma_start(
                out=rtg[:, cc * 1024:(cc + 1) * 1024],
                in_=y1b_scr[t0:t0 + HN, cc * 128:(cc + 1) * 128], transpose=True),
                reads=y1src, writes=[r_rtg[cc]], dma=True)
        for jj in range(HT):
            j = half * HT + jj
            m0 = rmx[:, 2 * j:2 * j + 1]
            m1 = rmx[:, 2 * j + 1:2 * j + 2]
            si = next_slot()
            emits = [lambda t, cc=cc, jj=jj, si=si: t.matmul(
                slot_ap(si, 0, 128, 0, 8),
                rtg[:, cc * 1024 + jj * 128:cc * 1024 + (jj + 1) * 128],
                RwT[:, cc * E:(cc + 1) * E],
                start=(cc == 0), stop=(cc == CT - 1)) for cc in range(CT)]
            P.mm_group(emits, reads=r_rtg + [r_RwT], writes=[slotR[si]])
            pr = probs[:, jj * E:(jj + 1) * E]
            rp = r_probs[jj]
            P.do("vector", lambda v, si=si, m0=m0: v.reduce_max(
                out=m0, in_=slot_ap(si, 0, 128, 0, 8), axis=AX.X, negate=True),
                reads=[slotR[si]], writes=[r_rmx[j]])
            P.do("scalar", lambda s, pr=pr, si=si, m0=m0, m1=m1: s.activation(
                out=pr, in_=slot_ap(si, 0, 128, 0, 8), func=AF.Exp, bias=m0,
                accum_out=m1),
                reads=[slotR[si], r_rmx[j]], writes=[rp, r_rmx[j]])
            P.do("vector", lambda v, m1=m1: v.reciprocal(out=m1, in_=m1),
                 reads=[], writes=[r_rmx[j]])
            P.do("vector", lambda v, pr=pr, m1=m1: v.tensor_scalar(
                out=pr, in0=pr, scalar1=m1, scalar2=None, op0=ALU.mult),
                reads=[r_rmx[j]], writes=[rp])
            ge = ge1[:, 0:E] if j % 2 == 0 else ge1[:, E:2 * E]
            rge = r_ge1[j % 2]
            P.do("vector", lambda v, pr=pr, m0=m0: v.reduce_max(
                out=m0, in_=pr, axis=AX.X),
                reads=[rp], writes=[r_rmx[j]])
            P.do("vector", lambda v, pr=pr, ge=ge, m0=m0: v.tensor_scalar(
                out=ge, in0=pr, scalar1=m0, scalar2=None, op0=ALU.is_ge),
                reads=[rp, r_rmx[j]], writes=[rge])
            P.do("vector", lambda v, pr=pr, ge=ge: v.scalar_tensor_tensor(
                out=ge, in0=ge, scalar=-1e9, in1=pr, op0=ALU.mult, op1=ALU.add),
                reads=[rp], writes=[rge])
            P.do("vector", lambda v, ge=ge, m0=m0: v.reduce_max(
                out=m0, in_=ge, axis=AX.X),
                reads=[rge], writes=[r_rmx[j]])
            P.do("vector", lambda v, pr=pr, ge=ge, m0=m0: v.tensor_scalar(
                out=ge, in0=pr, scalar1=m0, scalar2=None, op0=ALU.is_ge),
                reads=[rp, r_rmx[j]], writes=[rge])
            P.do("vector", lambda v, pr=pr, ge=ge, j=j: v.tensor_tensor(
                out=wte[:, j * E:(j + 1) * E], in0=pr, in1=ge, op=ALU.mult),
                reads=[rp, rge], writes=[r_wte[j]])
            P.do("vector", lambda v, j=j: v.tensor_single_scalar(
                out=maskt[:, j * E:(j + 1) * E], in_=wte[:, j * E:(j + 1) * E],
                scalar=0.0, op=ALU.is_gt),
                reads=[r_wte[j]], writes=[r_maskt[j]])
            sp = next_slot()
            P.mm_group([lambda t, j=j, sp=sp: t.matmul(
                slot_ap(sp, 0, 128, 0, 8), U128[:], maskt[:, j * E:(j + 1) * E],
                start=True, stop=True)],
                reads=[r_U, r_maskt[j]], writes=[slotR[sp]])
            sc = next_slot()
            P.mm_group([lambda t, j=j, sc=sc: t.matmul(
                slot_ap(sc, 0, 1, 0, 8), onescol[:], maskt[:, j * E:(j + 1) * E],
                start=True, stop=True)],
                reads=[r_onescol, r_maskt[j]], writes=[slotR[sc]])
            P.do("vector", lambda v, j=j: v.tensor_copy(
                out=offsall[0:1, j * E:(j + 1) * E], in_=wrow[0:1, 0:E]),
                reads=[r_wrow], writes=[r_offsall])
            P.do("vector", lambda v, sc=sc: v.tensor_tensor(
                out=wrow[0:1, 0:E], in0=slot_ap(sc, 0, 1, 0, 8), in1=wrow[0:1, 0:E],
                op=ALU.add), reads=[slotR[sc]], writes=[r_wrow])
            sb = next_slot()
            P.mm_group([lambda t, j=j, sb=sb: t.matmul(
                slot_ap(sb, 0, 128, 0, 8), ones_f[0:1, 0:128],
                offsall[0:1, j * E:(j + 1) * E], start=True, stop=True)],
                reads=[r_ones, r_offsall], writes=[slotR[sb]])
            P.do("vector", lambda v, sp=sp: v.tensor_copy(
                out=postile[:], in_=slot_ap(sp, 0, 128, 0, 8)),
                reads=[slotR[sp]], writes=[r_postile])
            P.do("vector", lambda v, sb=sb: v.scalar_tensor_tensor(
                out=tmpa[:], in0=slot_ap(sb, 0, 128, 0, 8), scalar=-1.0, in1=postile[:],
                op0=ALU.add, op1=ALU.add),
                reads=[slotR[sb], r_postile], writes=[r_tmpa])
            P.do("vector", lambda v: v.tensor_scalar(
                out=tmpa[:], in0=tmpa[:], scalar1=float(CAP), scalar2=None, op0=ALU.min),
                reads=[], writes=[r_tmpa])
            P.do("vector", lambda v, j=j: v.scalar_tensor_tensor(
                out=tmpb[:], in0=tmpa[:], scalar=-float(CAP), in1=maskt[:, j * E:(j + 1) * E],
                op0=ALU.add, op1=ALU.mult),
                reads=[r_tmpa, r_maskt[j]], writes=[r_tmpb])
            P.do("vector", lambda v, j=j: v.tensor_scalar(
                out=addri[:, j * E:(j + 1) * E], in0=tmpb[:], scalar1=float(CAP),
                scalar2=None, op0=ALU.add),
                reads=[r_tmpb], writes=[r_addr[j]])
        allz = [t for zz in ybuf_zero_tok for t in zz]
        for jj in range(HT):
            j = half * HT + jj
            jr = j % 2
            P.do("scalar", lambda s, j=j, jr=jr: s.dma_start(
                out=ysc[:, jr * 1024:(jr + 1) * 1024],
                in_=y1b_scr[j * 128:(j + 1) * 128, :]),
                reads=[r_y1bscr[j]], writes=[r_ysc[jr]], extra=allz, dma=True)
            for e in range(E):
                t = P.do("gpsimd", lambda g, e=e, j=j, jr=jr: g.indirect_dma_start(
                    out=ybuf_scr,
                    out_offset=IndirectOffsetOnAxis(addri[:, j * E + e:j * E + e + 1], 0),
                    in_=ysc[:, jr * 1024:(jr + 1) * 1024],
                    in_offset=None, element_offset=e * CAPP * C),
                    reads=[r_addr[j], r_ysc[jr]], dma=True)
                ybuf_ready[e].append(t)
    scat_done = sorted(set(t for tt in ybuf_ready for t in tt))
    for rr in r_yetm:
        rr.init_deps.extend(scat_done)
    rt_done = []
    for rr in r_rtg:
        if rr.w is not None:
            rt_done.append(rr.w)
        rt_done.extend(t for t in rr.readers if t is not None)
    for rr in r_Hg:
        rr.init_deps.extend(sorted(set(rt_done)))

    wbuf_idx = 0
    for e in range(E):
        yb = e % 2
        y0 = yb * CT * CAP
        ry = r_y1g[yb]
        if e + 2 < E:
            do_cast(e + 2)
        gb = wbuf_idx % 2; wbuf_idx += 1
        ub = wbuf_idx % 2; wbuf_idx += 1
        db = wbuf_idx % 2; wbuf_idx += 1
        Wg, Wu, Wd = (W1 if gb else W0), (W1 if ub else W0), (W1 if db else W0)
        rWg, rWu, rWd = r_W[gb], r_W[ub], r_W[db]

        for cc in range(CT):
            P.do("sync", lambda s, cc=cc, e=e, y0=y0: s.dma_start(
                out=y1g[:, y0 + cc * CAP:y0 + (cc + 1) * CAP],
                in_=ybuf_scr[e * CAPP:e * CAPP + CAP, cc * 128:(cc + 1) * 128],
                transpose=True),
                reads=[], writes=[ry[cc]], extra=ybuf_ready[e], dma=True)
        if e != 0:
            load_gateup(e, Wg, rWg, gate_scr, r_gscr[e])
            load_gateup(e, Wu, rWu, up_scr, r_uscr[e])

        for s_ in range(ST):
            m0 = rmx[:, 2 * s_:2 * s_ + 1]
            m1 = rmx[:, 2 * s_ + 1:2 * s_ + 2]
            si = next_slot()
            emits = [lambda t, cc=cc, s_=s_, si=si, y0=y0: t.matmul(
                slot_ap(si, 0, 128, 0, 8),
                y1g[:, y0 + cc * CAP + s_ * 128:y0 + cc * CAP + (s_ + 1) * 128],
                RwT[:, cc * E:(cc + 1) * E],
                start=(cc == 0), stop=(cc == CT - 1)) for cc in range(CT)]
            P.mm_group(emits, reads=ry + [r_RwT], writes=[slotR[si]])
            pr = probs[:, s_ * E:(s_ + 1) * E]
            rp = r_probs[s_]
            P.do("vector", lambda v, si=si, m0=m0: v.reduce_max(
                out=m0, in_=slot_ap(si, 0, 128, 0, 8), axis=AX.X, negate=True),
                reads=[slotR[si]], writes=[r_rmx[s_]])
            P.do("scalar", lambda s, pr=pr, si=si, m0=m0, m1=m1: s.activation(
                out=pr, in_=slot_ap(si, 0, 128, 0, 8), func=AF.Exp, bias=m0,
                accum_out=m1),
                reads=[slotR[si], r_rmx[s_]], writes=[rp, r_rmx[s_]])
            P.do("vector", lambda v, m1=m1: v.reciprocal(out=m1, in_=m1),
                 reads=[], writes=[r_rmx[s_]])
            P.do("vector", lambda v, pr=pr, m1=m1: v.tensor_scalar(
                out=pr, in0=pr, scalar1=m1, scalar2=None, op0=ALU.mult),
                reads=[r_rmx[s_]], writes=[rp])
            s1i = next_slot()
            P.mm_group([lambda t, s_=s_, e=e, s1i=s1i: t.matmul(
                slot_ap(s1i, 0, 1, 0, 128), probs[:, s_ * E + e:s_ * E + e + 1],
                ident_f[:], start=True, stop=True)],
                reads=[rp, r_idf], writes=[slotR[s1i]])
            P.do("vector", lambda v, s1i=s1i: v.tensor_copy(
                out=wrow[:], in_=slot_ap(s1i, 0, 1, 0, 128)),
                reads=[slotR[s1i]], writes=[r_wrow])
            s2i = next_slot()
            P.mm_group([lambda t, s2i=s2i: t.matmul(
                slot_ap(s2i, 0, 128, 0, 128), ones_f[0:1, 0:128],
                wrow[:], start=True, stop=True)],
                reads=[r_wrow, r_ones], writes=[slotR[s2i]])
            P.do("vector", lambda v, s_=s_, s2i=s2i: v.tensor_copy(
                out=bc_sb[:, s_ * 128:(s_ + 1) * 128],
                in_=slot_ap(s2i, 0, 128, 0, 128)),
                reads=[slotR[s2i]], writes=[r_bc])

        for ii in range(IT):
            pi = ii % 2
            prA, prB = pairR(pi)
            emits = []
            pre = []
            for (tco, tcl) in TCS:
                for cc in range(CT):
                    emits.append(lambda t, ii=ii, tco=tco, tcl=tcl, cc=cc, pi=pi, Wg=Wg, y0=y0: t.matmul(
                        pair_ap(pi, tco, tco + tcl),
                        Wg[:, cc * I + ii * 128:cc * I + (ii + 1) * 128],
                        y1g[:, y0 + cc * CAP + tco:y0 + cc * CAP + tco + tcl],
                        start=(cc == 0), stop=(cc == CT - 1)))
                    pre.append([rWg[cc], ry[cc]])
            P.mm_group(emits, writes=[prA, prB], per_reads=pre, reg_reads=rWg[CT:])
            P.do("scalar", lambda s, ii=ii, pi=pi: s.activation(
                out=Hg[:, ii * CAP:(ii + 1) * CAP], in_=pair_ap(pi, 0, CAP),
                func=AF.Silu),
                reads=[prA, prB], writes=[r_Hg[ii]])

        for ii in range(IT):
            P.do("scalar", lambda s, ii=ii, e=e, Wd=Wd: s.dma_start(
                out=Wd[:, ii * 1024:(ii + 1) * 1024],
                in_=down_scr[e * C:(e + 1) * C, ii * 128:(ii + 1) * 128],
                transpose=True),
                reads=[r_dscr[e]], writes=[rWd[ii]], dma=True)

        for ii in range(IT):
            pi = 2 + ii % 2
            prA, prB = pairR(pi)
            emits = []
            pre = []
            for (tco, tcl) in TCS:
                for cc in range(CT):
                    emits.append(lambda t, ii=ii, tco=tco, tcl=tcl, cc=cc, pi=pi, Wu=Wu, y0=y0: t.matmul(
                        pair_ap(pi, tco, tco + tcl),
                        Wu[:, cc * I + ii * 128:cc * I + (ii + 1) * 128],
                        y1g[:, y0 + cc * CAP + tco:y0 + cc * CAP + tco + tcl],
                        start=(cc == 0), stop=(cc == CT - 1)))
                    pre.append([rWu[cc], ry[cc]])
            P.mm_group(emits, writes=[prA, prB], per_reads=pre, reg_reads=rWu[CT:])
            hs = ii % 2
            P.do("vector", lambda v, hs=hs, pi=pi: v.tensor_tensor(
                out=Hu[:, hs * CAP:(hs + 1) * CAP], in0=pair_ap(pi, 0, CAP),
                in1=bc_sb[:, 0:CAP], op=ALU.mult),
                reads=[prA, prB, r_bc], writes=[r_Hu[hs]])
            heng = "gpsimd" if ii % 2 == 0 else "vector"
            P.do(heng, lambda g, ii=ii, hs=hs: g.tensor_tensor(
                out=Hg[:, ii * CAP:(ii + 1) * CAP],
                in0=Hg[:, ii * CAP:(ii + 1) * CAP],
                in1=Hu[:, hs * CAP:(hs + 1) * CAP], op=ALU.mult),
                reads=[r_Hu[hs]], writes=[r_Hg[ii]])

        for h in range(CT):
            for tci, (tco, tcl) in enumerate(TCS):
                si = next_slot()
                emits = [lambda t, ii=ii, h=h, tco=tco, tcl=tcl, si=si, Wd=Wd: t.matmul(
                    slot_ap(si, 0, 128, 0, tcl),
                    Wd[:, ii * 1024 + h * 128:ii * 1024 + (h + 1) * 128],
                    Hg[:, ii * CAP + tco:ii * CAP + tco + tcl],
                    start=(ii == 0), stop=(ii == IT - 1)) for ii in range(IT)]
                pre = [[rWd[ii], r_Hg[ii]] for ii in range(IT)]
                P.mm_group(emits, writes=[slotR[si]], per_reads=pre)
                P.do("vector", lambda v, h=h, tco=tco, tcl=tcl, si=si: v.tensor_copy(
                    out=Ye_sb[:, h * CAP + tco:h * CAP + tco + tcl],
                    in_=slot_ap(si, 0, 128, 0, tcl)),
                    reads=[slotR[si]], writes=[r_Yesb[h][tci]])

        for s_ in range(ST):
            sr = s_ % 2
            yreads = [r_Yesb[ccf][0 if s_ * 128 < 512 else 1] for ccf in range(CT)]
            for ccf in range(CT):
                si = next_slot()
                P.mm_group([lambda t, ccf=ccf, s_=s_, si=si: t.transpose(
                    slot_ap(si, 0, 128, 0, 128),
                    Ye_sb[:, ccf * CAP + s_ * 128:ccf * CAP + (s_ + 1) * 128],
                    ident_f[:])],
                    reads=[yreads[ccf], r_idf], writes=[slotR[si]])
                P.do("scalar", lambda s, ccf=ccf, sr=sr, si=si: s.copy(
                    out=yetm[:, sr * 1024 + ccf * 128:sr * 1024 + (ccf + 1) * 128],
                    in_=slot_ap(si, 0, 128, 0, 128)),
                    reads=[slotR[si]], writes=[r_yetm[sr]])
            P.do("scalar", lambda s, e=e, s_=s_, sr=sr: s.dma_start(
                out=ye_scr[e * CAPP + s_ * 128:e * CAPP + (s_ + 1) * 128, :],
                in_=yetm[:, sr * 1024:(sr + 1) * 1024]),
                reads=[r_yetm[sr]], writes=[r_Yescr[e]], dma=True)

        for j in range(NT):
            jr = j % 2
            P.do("gpsimd", lambda g, e=e, j=j, jr=jr: g.indirect_dma_start(
                out=yetm[:, jr * 1024:(jr + 1) * 1024],
                out_offset=None,
                in_=ye_scr,
                in_offset=IndirectOffsetOnAxis(addri[:, j * E + e:j * E + e + 1], 0),
                element_offset=e * CAPP * C),
                reads=[r_addr[j], r_Yescr[e]], writes=[r_yetm[jr]], dma=True)
            P.do("gpsimd", lambda g, j=j, jr=jr, e=e: g.dma_start(
                out=ytok_scr[j * 128:(j + 1) * 128, :],
                in_=yetm[:, jr * 1024:(jr + 1) * 1024],
                accum_op=(ALU.bypass if e == 0 else ALU.add)),
                reads=[r_yetm[jr]], writes=[r_Ytok[j]], dma=True)

    for j in range(NT):
        sl = j % 2
        lt = ln2t[:, sl * 1024:(sl + 1) * 1024]
        rlt = r_ln2t[sl]
        P.do("scalar", lambda s, j=j, lt=lt: s.dma_start(
            out=lt, in_=ytok_scr[j * 128:(j + 1) * 128, :]),
            reads=[r_Ytok[j]], writes=[rlt], dma=True)
        P.do("scalar", lambda s, j=j: s.dma_start(
            out=y1r[:], in_=y1f_scr[j * 128:(j + 1) * 128, :]),
            reads=[r_y1fscr[j]], writes=[r_y1r], dma=True)
        P.do("vector", lambda v, lt=lt: v.tensor_tensor(
            out=lt, in0=lt, in1=y1r[:], op=ALU.add),
            reads=[r_y1r], writes=[rlt])
        P.do("vector", lambda v, j=j, lt=lt: v.reduce_sum(
            out=sumc[:, j:j + 1], in_=lt, axis=AX.X),
            reads=[rlt], writes=[r_sumc[j]])
        P.do("scalar", lambda s, j=j, lt=lt: s.activation(
            out=sqd[:], in_=lt, func=AF.Square, accum_out=sqc[:, j:j + 1]),
            reads=[rlt], writes=[r_sqd, r_sqc[j]])
        P.do("vector", lambda v, j=j: v.tensor_scalar(
            out=meanc[:, j:j + 1], in0=sumc[:, j:j + 1], scalar1=1.0 / C,
            scalar2=None, op0=ALU.mult), reads=[r_sumc[j]], writes=[r_meanc[j]])
        P.do("vector", lambda v, j=j: v.tensor_tensor(
            out=tmpc[:, j:j + 1], in0=meanc[:, j:j + 1], in1=meanc[:, j:j + 1],
            op=ALU.mult), reads=[r_meanc[j]], writes=[r_tmpc[j]])
        P.do("vector", lambda v, j=j: v.scalar_tensor_tensor(
            out=varc[:, j:j + 1], in0=sqc[:, j:j + 1], scalar=1.0 / C,
            in1=tmpc[:, j:j + 1], op0=ALU.mult, op1=ALU.subtract),
            reads=[r_sqc[j], r_tmpc[j]], writes=[r_varc[j]])
        P.do("scalar", lambda s, j=j: s.activation(
            out=stdc[:, j:j + 1], in_=varc[:, j:j + 1], func=AF.Sqrt, bias=epsc[:]),
            reads=[r_varc[j], r_eps], writes=[r_stdc[j]])
        P.do("vector", lambda v, j=j: v.reciprocal(
            out=rstdc[:, j:j + 1], in_=stdc[:, j:j + 1]),
            reads=[r_stdc[j]], writes=[r_rstdc[j]])
        P.do("vector", lambda v, j=j, lt=lt: v.tensor_scalar(
            out=lt, in0=lt, scalar1=meanc[:, j:j + 1],
            scalar2=rstdc[:, j:j + 1], op0=ALU.subtract, op1=ALU.mult),
            reads=[r_meanc[j], r_rstdc[j]], writes=[rlt])
        P.do("scalar", lambda s, j=j, lt=lt: s.dma_start(
            out=out_d[j * 128:(j + 1) * 128, :], in_=lt),
            reads=[rlt], dma=True)

    final = P.counts["scalar"]
    P._op("scalar", lambda s: s.nop(), [("scalar", final)], sig=False)
    P.emit()
    return nc


def kernel(L0_attn_x, L0_attn_w_qkv, L0_attn_w_conv1, L0_ffn_router_w,
           L0_ffn_gate_w, L0_ffn_up_w, L0_ffn_down_w, _upto="full"):
    x = np.ascontiguousarray(np.asarray(L0_attn_x, dtype=np.float32))
    wqkv = np.ascontiguousarray(np.asarray(L0_attn_w_qkv, dtype=np.float32))
    wconv = np.ascontiguousarray(np.asarray(L0_attn_w_conv1, dtype=np.float32)).reshape(1, C)
    router = np.ascontiguousarray(np.asarray(L0_ffn_router_w, dtype=np.float32))
    gate = np.ascontiguousarray(np.asarray(L0_ffn_gate_w, dtype=np.float32)).reshape(E * I, C)
    up = np.ascontiguousarray(np.asarray(L0_ffn_up_w, dtype=np.float32)).reshape(E * I, C)
    down = np.ascontiguousarray(np.asarray(L0_ffn_down_w, dtype=np.float32)).reshape(E * C, I)

    nc = build(_upto)
    in_maps = []
    for b in range(8):
        in_maps.append({
            "x": np.ascontiguousarray(x[b]),
            "wqkv": wqkv, "wconv": wconv, "router": router,
            "gate": gate, "up": up, "down": down,
        })
    res = run_bass_kernel_spmd(nc, in_maps, core_ids=list(range(8)))
    return np.stack([res.results[b]["out"] for b in range(8)], axis=0)


# version 14
# speedup vs baseline: 1.0903x; 1.0903x over previous
import numpy as np
import concourse.bass as bass
import concourse.mybir as mybir
from concourse.bass_core_rust import IndirectOffsetOnAxis
from concourse.bass_utils import run_bass_kernel_spmd

F32 = mybir.dt.float32
BF16 = mybir.dt.bfloat16
AF = mybir.ActivationFunctionType
ALU = mybir.AluOpType
AX = mybir.AxisListType

N, C = 2048, 1024
NT = 16
CT = 8
E, I = 8, 2816
IT = 22
NH = 2
HN = N // NH
HT = NT // NH
CAP = 768
CAPP = 1024
ST = CAP // 128
TCS = [(0, 512), (512, CAP - 512)]
SCALE = float(C) ** -0.5
EPS = 1e-5

ENGS = ("sync", "tensor", "vector", "scalar", "gpsimd")


class R:
    __slots__ = ("w", "readers", "init_deps", "name")

    def __init__(self, name="r"):
        self.w = None
        self.readers = []
        self.init_deps = []
        self.name = name


class Prog:
    def __init__(self, nc):
        self.nc = nc
        self.ops = {e: [] for e in ENGS}
        self.sems = {e: nc.alloc_semaphore("sem_" + e) for e in ENGS}
        self.counts = {e: 0 for e in ENGS}
        self.last_wait = {e: {} for e in ENGS}

    def _op(self, eng, emit, toks, sig=True, dma=False):
        lw = self.last_wait[eng]
        waits = []
        for (s, v) in sorted(set(toks)):
            if s == eng and v > self.counts[eng]:
                raise AssertionError(f"future self-wait {s} {v}")
            if lw.get(s, 0) < v:
                waits.append((s, v))
                lw[s] = v
        inc = 16 if dma else (1 if sig else 0)
        tok = None
        if inc:
            self.counts[eng] += inc
            tok = (eng, self.counts[eng])
        self.ops[eng].append((waits, emit, self.sems[eng] if inc else None, inc))
        return tok

    def _gather(self, reads, writes, extra):
        toks = [t for t in extra if t is not None]
        for r in reads:
            if r.w is not None:
                toks.append(r.w)
            toks.extend(r.init_deps)
            r.init_deps = []
        for r in writes:
            if r.w is not None:
                toks.append(r.w)
            toks.extend(t for t in r.readers if t is not None)
            toks.extend(r.init_deps)
            r.init_deps = []
        return toks

    def do(self, eng, emit, reads=(), writes=(), extra=(), dma=False):
        toks = self._gather(reads, writes, extra)
        tok = self._op(eng, emit, toks, sig=True, dma=dma)
        for r in reads:
            r.readers.append(tok)
        for r in writes:
            r.w = tok
            r.readers = []
        return tok

    def mm_group(self, emits, reads=(), writes=(), extra=(), per_reads=None,
                 reg_reads=()):
        toks = self._gather(reads, writes, extra)
        n = len(emits)
        tok = None
        allr = list(reads)
        for i, e in enumerate(emits):
            ti = list(toks) if i == 0 else []
            if per_reads is not None:
                for r in per_reads[i]:
                    if r.w is not None:
                        ti.append(r.w)
                    ti.extend(r.init_deps)
                    r.init_deps = []
                    allr.append(r)
            t = self._op("tensor", e, ti, sig=(i == n - 1))
            if t is not None:
                tok = t
        allr.extend(reg_reads)
        for r in allr:
            r.readers.append(tok)
        for r in writes:
            r.w = tok
            r.readers = []
        return tok

    def emit(self):
        nc = self.nc
        with nc.Block() as block:
            for eng_name in ENGS:
                sems = self.sems

                def make(oplist):
                    def run(eng):
                        for waits, emit, inc_sem, inc_amt in oplist:
                            for (s, v) in waits:
                                eng.wait_ge(sems[s], v)
                            inst = emit(eng)
                            if inc_sem is not None:
                                inst.then_inc(inc_sem, inc_amt)
                    return run

                getattr(block, eng_name)(make(self.ops[eng_name]))


def build(upto="full", n_experts=E):
    nc = bass.Bass("TRN2", target_bir_lowering=False)

    x_d = nc.dram_tensor("x", [N, C], F32, kind="ExternalInput").ap()
    wqkv_d = nc.dram_tensor("wqkv", [3 * C, C], F32, kind="ExternalInput").ap()
    wconv_d = nc.dram_tensor("wconv", [1, C], F32, kind="ExternalInput").ap()
    router_d = nc.dram_tensor("router", [E, C], F32, kind="ExternalInput").ap()
    gate_d = nc.dram_tensor("gate", [E * I, C], F32, kind="ExternalInput").ap()
    up_d = nc.dram_tensor("up", [E * I, C], F32, kind="ExternalInput").ap()
    down_d = nc.dram_tensor("down", [E * C, I], F32, kind="ExternalInput").ap()
    out_d = nc.dram_tensor("out", [N, C], F32, kind="ExternalOutput").ap()

    gate_scr = nc.dram_tensor("gate_scr", [E * I, C], BF16, kind="Internal").ap()
    up_scr = nc.dram_tensor("up_scr", [E * I, C], BF16, kind="Internal").ap()
    down_scr = nc.dram_tensor("down_scr", [E * C, I], BF16, kind="Internal").ap()
    y1b_scr = nc.dram_tensor("y1b_scr", [N, C], BF16, kind="Internal").ap()
    y1f_scr = nc.dram_tensor("y1f_scr", [N, C], F32, kind="Internal").ap()
    ybuf_scr = nc.dram_tensor("ybuf_scr", [E * CAPP, C], BF16, kind="Internal").ap()
    ye_scr = nc.dram_tensor("ye_scr", [E * CAPP, C], F32, kind="Internal").ap()

    ident_f = nc.alloc_sbuf_tensor("ident_f", [128, 128], F32)
    ones_f = nc.alloc_sbuf_tensor("ones_f", [1, 128], F32)
    wrow = nc.alloc_sbuf_tensor("wrow", [1, 128], F32)
    bc_sb = nc.alloc_sbuf_tensor("bc_sb", [128, 1024], F32)
    sqd = nc.alloc_sbuf_tensor("sqd", [128, 1024], F32)
    sumc = nc.alloc_sbuf_tensor("sumc", [128, NT], F32)
    sqc = nc.alloc_sbuf_tensor("sqc", [128, NT], F32)
    meanc = nc.alloc_sbuf_tensor("meanc", [128, NT], F32)
    rstdc = nc.alloc_sbuf_tensor("rstdc", [128, NT], F32)
    tmpc = nc.alloc_sbuf_tensor("tmpc", [128, NT], F32)
    varc = nc.alloc_sbuf_tensor("varc", [128, NT], F32)
    stdc = nc.alloc_sbuf_tensor("stdc", [128, NT], F32)
    agc = nc.alloc_sbuf_tensor("agc", [128, NT], F32)
    s2c = nc.alloc_sbuf_tensor("s2c", [128, NT], F32)
    w2c = nc.alloc_sbuf_tensor("w2c", [128, NT], F32)
    mxc = nc.alloc_sbuf_tensor("mxc", [128, 4], F32)
    epsc = nc.alloc_sbuf_tensor("epsc", [128, 1], F32)
    cols_bf = nc.alloc_sbuf_tensor("cols_bf", [128, 8], BF16)
    W0 = nc.alloc_sbuf_tensor("W0", [128, IT * 1024], BF16)
    W1 = nc.alloc_sbuf_tensor("W1", [128, IT * 1024], BF16)
    RwT = nc.alloc_sbuf_tensor("RwT", [128, CT * E], BF16)

    rem = nc.sbuf_bytes_remaining
    top = 229344
    base = top - rem

    def at(name, shape, dtype, off):
        return nc.alloc_sbuf_tensor_at(name, shape, dtype, offset=base + off)

    x_tm = at("x_tm", [128, NT * 1024], F32, 0)
    qkv_buf = at("qkv_buf", [128, CT * 1024], BF16, 65536)
    y1b_st = at("y1b_st", [128, 2 * 1024], BF16, 65536 + 16384)
    rows = at("rows", [1, 2048], F32, 65536 + 16384 + 4096)
    wstg = at("wstg", [128, 2048], F32, 65536 + 16384 + 4096 + 8192)
    attn_end = 65536 + 16384 + 4096 + 8192 + 8192
    Hg = at("Hg", [128, IT * CAP], BF16, 0)
    o = IT * CAP * 2
    y1g = at("y1g", [128, 2 * CT * CAP], BF16, o); o += 2 * CT * CAP * 2
    Ye_sb = at("Ye_sb", [128, CT * CAP], F32, o); o += CT * CAP * 4
    Hu = at("Hu", [128, 2 * CAP], BF16, o); o += 2 * CAP * 2
    ln2t = at("ln2t", [128, 2048], F32, o); o += 8192
    y1r = at("y1r", [128, 1024], F32, o); o += 4096
    yetm = at("yetm", [128, 2048], F32, o)
    ysc = at("ysc", [128, 2048], BF16, o)
    o += 8192
    zero_t = at("zero_t", [128, 1024], BF16, o); o += 2048
    probs = at("probs", [128, HT * E], F32, o); o += 256
    wte = at("wte", [128, NT * E], F32, o); o += 512
    ge1 = at("ge1", [128, 2 * E], F32, o); o += 64
    addri = at("addri", [128, NT * E], mybir.dt.int32, o); o += 512
    U128 = at("U128", [128, 128], F32, o); o += 512
    maskt = at("maskt", [128, NT * E], F32, o); o += 512
    postile = at("postile", [128, E], F32, o); o += 64
    offsall = at("offsall", [1, 128], F32, o); o += 512
    onescol = at("onescol", [128, 1], F32, o); o += 32
    tmpa = at("tmpa", [128, E], F32, o); o += 64
    tmpb = at("tmpb", [128, E], F32, o); o += 64
    rmx = at("rmx", [128, 2 * NT], F32, o); o += 128
    ffn_end = o
    assert base + max(ffn_end, attn_end) <= top - 4, (base, ffn_end)

    psT = [nc.alloc_psum_tensor(f"ps{i}", [128, 1024], F32) for i in range(4)]

    P = Prog(nc)

    slot_t = [(psT[i // 2], (i % 2) * 512) for i in range(8)]
    slotR = [R(f"ps{i}") for i in range(8)]

    def slot_ap(i, p0, pn, f0, fn):
        t, off = slot_t[i]
        return t[p0:p0 + pn, off + f0:off + fn]

    def pair_ap(pi, f0, fn):
        return psT[pi][:, f0:fn]

    def pairR(pi):
        return (slotR[2 * pi], slotR[2 * pi + 1])

    xt = [R(f"x{j}") for j in range(NT)]
    r_qkv = [R(f"qkv{c}") for c in range(CT)]
    r_rows = [R("rows0"), R("rows1")]
    r_wstg = [R("wstg0"), R("wstg1")]
    r_wrow = R("wrow")
    r_bc = R("bc")
    r_sqd = R("sqd")
    r_agc, r_s2c, r_w2c = R("agc"), R("s2c"), R("w2c")
    r_mxc = R("mxc")
    r_colsbf = R("cols_bf")
    r_sumc = [R(f"sumc{j}") for j in range(NT)]
    r_sqc = [R(f"sqc{j}") for j in range(NT)]
    r_meanc = [R(f"meanc{j}") for j in range(NT)]
    r_rstdc = [R(f"rstdc{j}") for j in range(NT)]
    r_tmpc = [R(f"tmpc{j}") for j in range(NT)]
    r_varc = [R(f"varc{j}") for j in range(NT)]
    r_stdc = [R(f"stdc{j}") for j in range(NT)]
    r_y1bst = [R("y1b_st0"), R("y1b_st1")]
    r_idf, r_ones = R("ident_f"), R("ones_f")
    r_eps = R("epsc")
    r_W = [[R(f"W0_{i}") for i in range(IT)], [R(f"W1_{i}") for i in range(IT)]]
    r_RwT = R("RwT")
    r_gscr = [R(f"gscr{e}") for e in range(E)]
    r_uscr = [R(f"uscr{e}") for e in range(E)]
    r_dscr = [R(f"dscr{e}") for e in range(E)]
    r_y1bscr = [R(f"y1bscr{j}") for j in range(NT)]
    r_y1fscr = [R(f"y1fscr{j}") for j in range(NT)]
    r_Hg = [R(f"Hg{i}") for i in range(IT)]
    r_rtg = [R(f"rtg{c}") for c in range(CT)]
    r_y1g = [[R(f"y1g{b}_{c}") for c in range(CT)] for b in range(2)]
    r_rmx = [R(f"rmx{j}") for j in range(NT)]
    r_Yesb = [[R(f"Ye{h}_{t}") for t in range(len(TCS))] for h in range(CT)]
    r_Hu = [R("Hu0"), R("Hu1")]
    r_ln2t = [R("ln2t0"), R("ln2t1")]
    r_y1r = R("y1r")
    r_yetm = [R("yetm0"), R("yetm1")]
    r_ysc = [R("ysc0"), R("ysc1")]
    r_zero = R("zero_t")
    r_probs = [R(f"probs{j}") for j in range(HT)]
    r_wte = [R(f"wte{j}") for j in range(NT)]
    r_ge1 = [R("ge1a"), R("ge1b")]
    r_maskt = [R(f"maskt{j}") for j in range(NT)]
    r_addr = [R(f"addr{j}") for j in range(NT)]
    r_U, r_onescol, r_offsall = R("U128"), R("onescol"), R("offsall")
    r_postile, r_tmpa, r_tmpb = R("postile"), R("tmpa"), R("tmpb")
    r_Yescr = [R(f"Yescr{e}") for e in range(E)]

    P.do("gpsimd", lambda g: g.memset(ident_f[:], 0.0), writes=[r_idf])
    P.do("gpsimd", lambda g: g.affine_select(
        out=ident_f[:], in_=ident_f[:], compare_op=ALU.not_equal, fill=1.0,
        base=0, pattern=[[-1, 128]], channel_multiplier=1),
        writes=[r_idf])
    P.do("vector", lambda v: v.memset(ones_f[:], 1.0), writes=[r_ones])
    P.do("vector", lambda v: v.memset(epsc[:], EPS), writes=[r_eps])

    def do_cast(e, extra=()):
        P.do("gpsimd", lambda g, e=e: g.dma_start(
            out=gate_scr[e * I:(e + 1) * I, :], in_=gate_d[e * I:(e + 1) * I, :]),
            writes=[r_gscr[e]], extra=extra, dma=True)
        P.do("gpsimd", lambda g, e=e: g.dma_start(
            out=up_scr[e * I:(e + 1) * I, :], in_=up_d[e * I:(e + 1) * I, :]),
            writes=[r_uscr[e]], extra=extra, dma=True)
        P.do("gpsimd", lambda g, e=e: g.dma_start(
            out=down_scr[e * C:(e + 1) * C, :], in_=down_d[e * C:(e + 1) * C, :]),
            writes=[r_dscr[e]], extra=extra, dma=True)

    def load_gateup(e, W, rW, scr, rscr):
        for cc in range(CT):
            P.do("sync", lambda s, cc=cc, e=e, W=W: s.dma_start(
                out=W[:, cc * I:(cc + 1) * I],
                in_=scr[e * I:(e + 1) * I, cc * 128:(cc + 1) * 128],
                transpose=True),
                reads=[rscr], writes=[rW[cc]], dma=True)

    tok_x = P.do("scalar", lambda s: s.dma_start(
        out=x_tm[:].rearrange("p (j c) -> p j c", c=1024),
        in_=x_d.rearrange("(j p) c -> p j c", p=128)), dma=True)
    for j in range(NT):
        xt[j].w = tok_x
    P.do("scalar", lambda s: s.dma_start(out=rows[0:1, 0:1024], in_=wconv_d),
         writes=[r_rows[0]], dma=True)

    SL = [0]

    def next_slot():
        i = SL[0] % 8
        SL[0] += 1
        return i

    def next_pair():
        i0 = (SL[0] % 4) * 2
        SL[0] += 2
        return i0

    def row_to_cols_bf(row_slot):
        for c in range(CT):
            si = next_slot()
            P.mm_group([lambda t, c=c, si=si, rs=row_slot: t.matmul(
                slot_ap(si, 0, 128, 0, 1),
                rows[0:1, rs * 1024 + c * 128:rs * 1024 + (c + 1) * 128],
                ones_f[0:1, 0:1], start=True, stop=True)],
                reads=[r_rows[row_slot], r_ones], writes=[slotR[si]])
            P.do("scalar", lambda s, c=c, si=si: s.copy(
                out=cols_bf[:, c:c + 1], in_=slot_ap(si, 0, 128, 0, 1)),
                reads=[slotR[si]], writes=[r_colsbf])

    def broadcast_row_to_bc(row_slot):
        i0 = next_pair()
        for h in range(2):
            P.mm_group([lambda t, h=h, i0=i0, rs=row_slot: t.matmul(
                slot_ap(i0 + h, 0, 128, 0, 512),
                ones_f[0:1, 0:128],
                rows[0:1, rs * 1024 + h * 512:rs * 1024 + (h + 1) * 512],
                start=True, stop=True)],
                reads=[r_rows[row_slot], r_ones], writes=[slotR[i0 + h]])
        P.do("vector", lambda v, i0=i0: v.tensor_copy(
            out=bc_sb[:], in_=pair_ap(i0 // 2, 0, 1024)),
            reads=[slotR[i0], slotR[i0 + 1]], writes=[r_bc])

    def matvec_row(lhs_cols_res, lhs_col_ap, rhs_buf_ap, rhs_res, out_row_slot,
                   nk=CT, post_scale=None):
        i0 = next_pair()
        for h in range(2):
            emits = [lambda t, k=k, h=h, i0=i0: t.matmul(
                slot_ap(i0 + h, 0, 1, 0, 512),
                lhs_col_ap(k),
                rhs_buf_ap(k, h),
                start=(k == 0), stop=(k == nk - 1)) for k in range(nk)]
            P.mm_group(emits, reads=list(lhs_cols_res) + list(rhs_res),
                       writes=[slotR[i0 + h]])
        if post_scale is None:
            P.do("vector", lambda v, i0=i0, rs=out_row_slot: v.tensor_copy(
                out=rows[0:1, rs * 1024:(rs + 1) * 1024], in_=pair_ap(i0 // 2, 0, 1024)[0:1, :]),
                reads=[slotR[i0], slotR[i0 + 1]], writes=[r_rows[out_row_slot]])
        else:
            P.do("vector", lambda v, i0=i0, rs=out_row_slot, sc=post_scale: v.tensor_scalar(
                out=rows[0:1, rs * 1024:(rs + 1) * 1024], in0=pair_ap(i0 // 2, 0, 1024)[0:1, :],
                scalar1=sc, scalar2=None, op0=ALU.mult),
                reads=[slotR[i0], slotR[i0 + 1]], writes=[r_rows[out_row_slot]])

    def load_nat_bf(row0):
        for r in range(CT):
            pp = r % 2
            P.do("scalar", lambda s, r=r, pp=pp: s.dma_start(
                out=wstg[:, pp * 1024:(pp + 1) * 1024],
                in_=wqkv_d[row0 + r * 128:row0 + (r + 1) * 128, :]),
                writes=[r_wstg[pp]], dma=True)
            P.do("scalar", lambda s, r=r, pp=pp: s.copy(
                out=qkv_buf[:, r * 1024:(r + 1) * 1024],
                in_=wstg[:, pp * 1024:(pp + 1) * 1024]),
                reads=[r_wstg[pp]], writes=[r_qkv[r]])

    def load_T_bf(row0):
        for r in range(CT):
            pp = r % 2
            P.do("scalar", lambda s, r=r, pp=pp: s.dma_start(
                out=wstg[:, pp * 1024:(pp + 1) * 1024],
                in_=wqkv_d[row0 + r * 128:row0 + (r + 1) * 128, :]),
                writes=[r_wstg[pp]], dma=True)
            for t in range(CT):
                si = next_slot()
                P.mm_group([lambda tn, r=r, t=t, pp=pp, si=si: tn.transpose(
                    slot_ap(si, 0, 128, 0, 128),
                    wstg[:, pp * 1024 + t * 128:pp * 1024 + (t + 1) * 128],
                    ident_f[:])],
                    reads=[r_wstg[pp], r_idf], writes=[slotR[si]])
                P.do("scalar", lambda s, r=r, t=t, si=si: s.copy(
                    out=qkv_buf[:, t * 1024 + r * 128:t * 1024 + (r + 1) * 128],
                    in_=slot_ap(si, 0, 128, 0, 128)),
                    reads=[slotR[si]], writes=[r_qkv[t]])

    load_nat_bf(0)
    row_to_cols_bf(0)
    matvec_row([r_colsbf], lambda k: cols_bf[:, k:k + 1],
               lambda k, h: qkv_buf[:, k * 1024 + h * 512:k * 1024 + (h + 1) * 512],
               r_qkv, 1)
    broadcast_row_to_bc(1)
    for j in range(NT):
        P.do("vector", lambda v, j=j: v.scalar_tensor_tensor(
            out=sqd[:], in0=x_tm[:, j * 1024:(j + 1) * 1024], scalar=1.0,
            in1=bc_sb[:], op0=ALU.mult, op1=ALU.mult,
            accum_out=agc[:, j:j + 1]),
            reads=[xt[j], r_bc], writes=[r_sqd, r_agc])
    matvec_row([r_agc] + xt, lambda k: agc[:, k:k + 1],
               lambda k, h: x_tm[:, k * 1024 + h * 512:k * 1024 + (h + 1) * 512],
               [], 0, nk=NT, post_scale=SCALE)
    row_to_cols_bf(0)
    load_T_bf(C)
    do_cast(0, extra=[("scalar", P.counts["scalar"])])
    load_gateup(0, W0, r_W[0], gate_scr, r_gscr[0])
    matvec_row([r_colsbf], lambda k: cols_bf[:, k:k + 1],
               lambda k, h: qkv_buf[:, k * 1024 + h * 512:k * 1024 + (h + 1) * 512],
               r_qkv, 0)
    P.do("vector", lambda v: v.reduce_max(out=mxc[0:1, 0:1], in_=rows[0:1, 0:1024],
                                          axis=AX.X, negate=True),
         reads=[r_rows[0]], writes=[r_mxc])
    P.do("scalar", lambda s: s.activation(out=rows[0:1, 0:1024], in_=rows[0:1, 0:1024],
                                          func=AF.Exp, bias=mxc[0:1, 0:1]),
         reads=[r_mxc], writes=[r_rows[0]])
    P.do("vector", lambda v: v.reduce_sum(out=mxc[0:1, 1:2], in_=rows[0:1, 0:1024], axis=AX.X),
         reads=[r_rows[0]], writes=[r_mxc])
    P.do("vector", lambda v: v.reciprocal(out=mxc[0:1, 2:3], in_=mxc[0:1, 1:2]),
         reads=[r_mxc], writes=[r_mxc])
    P.do("vector", lambda v: v.tensor_scalar(out=rows[0:1, 0:1024], in0=rows[0:1, 0:1024],
                                             scalar1=mxc[0:1, 2:3], scalar2=None, op0=ALU.mult),
         reads=[r_mxc], writes=[r_rows[0]])
    row_to_cols_bf(0)
    load_nat_bf(2 * C)
    do_cast(1, extra=[("scalar", P.counts["scalar"])])
    load_gateup(0, W1, r_W[1], up_scr, r_uscr[0])
    matvec_row([r_colsbf], lambda k: cols_bf[:, k:k + 1],
               lambda k, h: qkv_buf[:, k * 1024 + h * 512:k * 1024 + (h + 1) * 512],
               r_qkv, 1)
    broadcast_row_to_bc(1)
    for j in range(NT):
        P.do("vector", lambda v, j=j: v.scalar_tensor_tensor(
            out=sqd[:], in0=x_tm[:, j * 1024:(j + 1) * 1024], scalar=1.0,
            in1=bc_sb[:], op0=ALU.mult, op1=ALU.mult,
            accum_out=s2c[:, j:j + 1]),
            reads=[xt[j], r_bc], writes=[r_sqd, r_s2c])
    P.do("vector", lambda v: v.reduce_max(out=mxc[:, 0:1], in_=s2c[:], axis=AX.X),
         reads=[r_s2c], writes=[r_mxc])
    si = next_slot()
    P.mm_group([lambda t, si=si: t.matmul(
        slot_ap(si, 0, 1, 0, 128), mxc[:, 0:1], ident_f[:], start=True, stop=True)],
        reads=[r_mxc, r_idf], writes=[slotR[si]])
    P.do("vector", lambda v, si=si: v.tensor_copy(out=wrow[:], in_=slot_ap(si, 0, 1, 0, 128)),
         reads=[slotR[si]], writes=[r_wrow])
    P.do("vector", lambda v: v.reduce_max(out=mxc[0:1, 1:2], in_=wrow[:],
                                          axis=AX.X, negate=True),
         reads=[r_wrow], writes=[r_mxc])
    si = next_slot()
    P.mm_group([lambda t, si=si: t.matmul(
        slot_ap(si, 0, 128, 0, 1), ones_f[0:1, 0:128], mxc[0:1, 1:2], start=True, stop=True)],
        reads=[r_mxc, r_ones], writes=[slotR[si]])
    P.do("vector", lambda v, si=si: v.tensor_copy(out=mxc[:, 2:3], in_=slot_ap(si, 0, 128, 0, 1)),
         reads=[slotR[si]], writes=[r_mxc])
    P.do("scalar", lambda s: s.activation(out=w2c[:], in_=s2c[:], func=AF.Exp,
                                          bias=mxc[:, 2:3]),
         reads=[r_s2c, r_mxc], writes=[r_w2c])
    P.do("vector", lambda v: v.reduce_sum(out=mxc[:, 0:1], in_=w2c[:], axis=AX.X),
         reads=[r_w2c], writes=[r_mxc])
    si = next_slot()
    P.mm_group([lambda t, si=si: t.matmul(
        slot_ap(si, 0, 1, 0, 128), mxc[:, 0:1], ident_f[:], start=True, stop=True)],
        reads=[r_mxc, r_idf], writes=[slotR[si]])
    P.do("vector", lambda v, si=si: v.tensor_copy(out=wrow[:], in_=slot_ap(si, 0, 1, 0, 128)),
         reads=[slotR[si]], writes=[r_wrow])
    P.do("vector", lambda v: v.reduce_sum(out=mxc[0:1, 1:2], in_=wrow[:], axis=AX.X),
         reads=[r_wrow], writes=[r_mxc])
    P.do("vector", lambda v: v.reciprocal(out=mxc[0:1, 1:2], in_=mxc[0:1, 1:2]),
         reads=[r_mxc], writes=[r_mxc])
    si = next_slot()
    P.mm_group([lambda t, si=si: t.matmul(
        slot_ap(si, 0, 128, 0, 1), ones_f[0:1, 0:128], mxc[0:1, 1:2], start=True, stop=True)],
        reads=[r_mxc, r_ones], writes=[slotR[si]])
    P.do("vector", lambda v, si=si: v.tensor_copy(out=mxc[:, 3:4], in_=slot_ap(si, 0, 128, 0, 1)),
         reads=[slotR[si]], writes=[r_mxc])
    P.do("vector", lambda v: v.tensor_scalar(out=w2c[:], in0=w2c[:], scalar1=mxc[:, 3:4],
                                             scalar2=None, op0=ALU.mult),
         reads=[r_mxc], writes=[r_w2c])
    matvec_row([r_w2c] + xt, lambda k: w2c[:, k:k + 1],
               lambda k, h: x_tm[:, k * 1024 + h * 512:k * 1024 + (h + 1) * 512],
               [], 0, nk=NT)
    row_to_cols_bf(0)
    load_T_bf(2 * C)
    matvec_row([r_colsbf], lambda k: cols_bf[:, k:k + 1],
               lambda k, h: qkv_buf[:, k * 1024 + h * 512:k * 1024 + (h + 1) * 512],
               r_qkv, 1)
    broadcast_row_to_bc(1)

    for j in range(NT):
        xa = x_tm[:, j * 1024:(j + 1) * 1024]
        P.do("vector", lambda v, xa=xa, j=j: v.scalar_tensor_tensor(
            out=xa, in0=xa, scalar=1.0, in1=bc_sb[:], op0=ALU.mult, op1=ALU.add,
            accum_out=sumc[:, j:j + 1]),
            reads=[r_bc], writes=[xt[j], r_sumc[j]])
        P.do("scalar", lambda s, xa=xa, j=j: s.activation(
            out=sqd[:], in_=xa, func=AF.Square, accum_out=sqc[:, j:j + 1]),
            reads=[xt[j]], writes=[r_sqd, r_sqc[j]])
        P.do("vector", lambda v, j=j: v.tensor_scalar(
            out=meanc[:, j:j + 1], in0=sumc[:, j:j + 1], scalar1=1.0 / C, scalar2=None,
            op0=ALU.mult), reads=[r_sumc[j]], writes=[r_meanc[j]])
        P.do("vector", lambda v, j=j: v.tensor_tensor(
            out=tmpc[:, j:j + 1], in0=meanc[:, j:j + 1], in1=meanc[:, j:j + 1],
            op=ALU.mult), reads=[r_meanc[j]], writes=[r_tmpc[j]])
        P.do("vector", lambda v, j=j: v.scalar_tensor_tensor(
            out=varc[:, j:j + 1], in0=sqc[:, j:j + 1], scalar=1.0 / C,
            in1=tmpc[:, j:j + 1], op0=ALU.mult, op1=ALU.subtract),
            reads=[r_sqc[j], r_tmpc[j]], writes=[r_varc[j]])
        P.do("scalar", lambda s, j=j: s.activation(
            out=stdc[:, j:j + 1], in_=varc[:, j:j + 1], func=AF.Sqrt, bias=epsc[:]),
            reads=[r_varc[j], r_eps], writes=[r_stdc[j]])
        P.do("vector", lambda v, j=j: v.reciprocal(
            out=rstdc[:, j:j + 1], in_=stdc[:, j:j + 1]),
            reads=[r_stdc[j]], writes=[r_rstdc[j]])
        P.do("vector", lambda v, xa=xa, j=j: v.tensor_scalar(
            out=xa, in0=xa, scalar1=meanc[:, j:j + 1], scalar2=rstdc[:, j:j + 1],
            op0=ALU.subtract, op1=ALU.mult),
            reads=[r_meanc[j], r_rstdc[j]], writes=[xt[j]])
        st = j % 2
        P.do("scalar", lambda s, xa=xa, st=st: s.copy(
            out=y1b_st[:, st * 1024:(st + 1) * 1024], in_=xa),
            reads=[xt[j]], writes=[r_y1bst[st]])
        P.do("scalar", lambda s, j=j, st=st: s.dma_start(
            out=y1b_scr[j * 128:(j + 1) * 128, :],
            in_=y1b_st[:, st * 1024:(st + 1) * 1024]),
            reads=[r_y1bst[st]], writes=[r_y1bscr[j]], dma=True)
        P.do("scalar", lambda s, xa=xa, j=j: s.dma_start(
            out=y1f_scr[j * 128:(j + 1) * 128, :], in_=xa),
            reads=[xt[j]], writes=[r_y1fscr[j]], dma=True)

    if upto == "y1":
        for j in range(NT):
            P.do("scalar", lambda s, j=j: s.dma_start(
                out=out_d[j * 128:(j + 1) * 128, :],
                in_=x_tm[:, j * 1024:(j + 1) * 1024]),
                reads=[xt[j]], dma=True)
        final = P.counts["scalar"]
        P._op("scalar", lambda s: s.nop(), [("scalar", final)], sig=False)
        P.emit()
        return nc

    attn_done = []
    for rr in xt + r_qkv + [r_rows[0], r_rows[1], r_y1bst[0], r_y1bst[1]]:
        if rr.w is not None:
            attn_done.append(rr.w)
        attn_done.extend(t for t in rr.readers if t is not None)
    attn_done = sorted(set(attn_done))
    overlay_rs = (r_ln2t + [r_y1r, r_Hu[0], r_Hu[1], r_ge1[0], r_ge1[1], r_zero, r_U,
                            r_onescol, r_offsall, r_postile, r_tmpa, r_tmpb]
                  + r_Hg + r_rtg + r_y1g[0] + r_y1g[1] + r_rmx
                  + r_probs + r_wte + r_maskt + r_addr
                  + r_yetm + r_ysc
                  + [r_Yesb[h][t] for h in range(CT) for t in range(len(TCS))])
    for rr in overlay_rs:
        rr.init_deps = list(attn_done)

    P.do("vector", lambda v: v.memset(zero_t[:], 0.0), writes=[r_zero])
    P.do("vector", lambda v: v.memset(onescol[:], 1.0), writes=[r_onescol])
    P.do("vector", lambda v: v.memset(offsall[:], 0.0), writes=[r_offsall])
    P.do("gpsimd", lambda g: g.memset(U128[:], 0.0), writes=[r_U])
    P.do("gpsimd", lambda g: g.affine_select(
        out=U128[:], in_=U128[:], compare_op=ALU.is_gt, fill=1.0,
        base=0, pattern=[[-1, 128]], channel_multiplier=1), writes=[r_U])

    ybuf_zero_tok = [[] for _ in range(E)]
    for e in range(E):
        for k in range(ST):
            t = P.do("sync", lambda s, e=e, k=k: s.dma_start(
                out=ybuf_scr[e * CAPP + k * 128:e * CAPP + (k + 1) * 128, :],
                in_=zero_t[:]), reads=[r_zero], dma=True)
            ybuf_zero_tok[e].append(t)
        P.do("gpsimd", lambda g, e=e: g.dma_start(
            out=ye_scr[e * CAPP + CAP:e * CAPP + CAP + 1, :],
            in_=zero_t[0:1, :]), reads=[r_zero], writes=[r_Yescr[e]], dma=True)

    P.do("scalar", lambda s: s.dma_start(out=bc_sb[0:E, :], in_=router_d),
         writes=[r_bc], dma=True)
    for cc in range(CT):
        si = next_slot()
        P.mm_group([lambda t, cc=cc, si=si: t.transpose(
            slot_ap(si, 0, 128, 0, 8), bc_sb[0:E, cc * 128:(cc + 1) * 128],
            ident_f[0:E, 0:E])],
            reads=[r_bc, r_idf], writes=[slotR[si]])
        P.do("scalar", lambda s, cc=cc, si=si: s.copy(
            out=RwT[:, cc * E:(cc + 1) * E], in_=slot_ap(si, 0, 128, 0, 8)),
            reads=[slotR[si]], writes=[r_RwT])

    do_cast(2)
    rtg = Hg
    ybuf_ready = [[] for _ in range(E)]
    for half in range(NH):
        t0 = half * HN
        y1src = [r_y1bscr[half * HT + jj] for jj in range(HT)]
        for cc in range(CT):
            P.do("scalar", lambda s, cc=cc, t0=t0: s.dma_start(
                out=rtg[:, cc * 1024:(cc + 1) * 1024],
                in_=y1b_scr[t0:t0 + HN, cc * 128:(cc + 1) * 128], transpose=True),
                reads=y1src, writes=[r_rtg[cc]], dma=True)
        for jj in range(HT):
            j = half * HT + jj
            m0 = rmx[:, 2 * j:2 * j + 1]
            m1 = rmx[:, 2 * j + 1:2 * j + 2]
            si = next_slot()
            emits = [lambda t, cc=cc, jj=jj, si=si: t.matmul(
                slot_ap(si, 0, 128, 0, 8),
                rtg[:, cc * 1024 + jj * 128:cc * 1024 + (jj + 1) * 128],
                RwT[:, cc * E:(cc + 1) * E],
                start=(cc == 0), stop=(cc == CT - 1)) for cc in range(CT)]
            P.mm_group(emits, reads=r_rtg + [r_RwT], writes=[slotR[si]])
            pr = probs[:, jj * E:(jj + 1) * E]
            rp = r_probs[jj]
            P.do("vector", lambda v, si=si, m0=m0: v.reduce_max(
                out=m0, in_=slot_ap(si, 0, 128, 0, 8), axis=AX.X, negate=True),
                reads=[slotR[si]], writes=[r_rmx[j]])
            P.do("scalar", lambda s, pr=pr, si=si, m0=m0, m1=m1: s.activation(
                out=pr, in_=slot_ap(si, 0, 128, 0, 8), func=AF.Exp, bias=m0,
                accum_out=m1),
                reads=[slotR[si], r_rmx[j]], writes=[rp, r_rmx[j]])
            P.do("vector", lambda v, m1=m1: v.reciprocal(out=m1, in_=m1),
                 reads=[], writes=[r_rmx[j]])
            P.do("vector", lambda v, pr=pr, m1=m1: v.tensor_scalar(
                out=pr, in0=pr, scalar1=m1, scalar2=None, op0=ALU.mult),
                reads=[r_rmx[j]], writes=[rp])
            ge = ge1[:, 0:E] if j % 2 == 0 else ge1[:, E:2 * E]
            rge = r_ge1[j % 2]
            P.do("vector", lambda v, pr=pr, m0=m0: v.reduce_max(
                out=m0, in_=pr, axis=AX.X),
                reads=[rp], writes=[r_rmx[j]])
            P.do("vector", lambda v, pr=pr, ge=ge, m0=m0: v.tensor_scalar(
                out=ge, in0=pr, scalar1=m0, scalar2=None, op0=ALU.is_ge),
                reads=[rp, r_rmx[j]], writes=[rge])
            P.do("vector", lambda v, pr=pr, ge=ge: v.scalar_tensor_tensor(
                out=ge, in0=ge, scalar=-1e9, in1=pr, op0=ALU.mult, op1=ALU.add),
                reads=[rp], writes=[rge])
            P.do("vector", lambda v, ge=ge, m0=m0: v.reduce_max(
                out=m0, in_=ge, axis=AX.X),
                reads=[rge], writes=[r_rmx[j]])
            P.do("vector", lambda v, pr=pr, ge=ge, m0=m0: v.tensor_scalar(
                out=ge, in0=pr, scalar1=m0, scalar2=None, op0=ALU.is_ge),
                reads=[rp, r_rmx[j]], writes=[rge])
            P.do("vector", lambda v, pr=pr, ge=ge, j=j: v.tensor_tensor(
                out=wte[:, j * E:(j + 1) * E], in0=pr, in1=ge, op=ALU.mult),
                reads=[rp, rge], writes=[r_wte[j]])
            P.do("vector", lambda v, j=j: v.tensor_single_scalar(
                out=maskt[:, j * E:(j + 1) * E], in_=wte[:, j * E:(j + 1) * E],
                scalar=0.0, op=ALU.is_gt),
                reads=[r_wte[j]], writes=[r_maskt[j]])
            sp = next_slot()
            P.mm_group([lambda t, j=j, sp=sp: t.matmul(
                slot_ap(sp, 0, 128, 0, 8), U128[:], maskt[:, j * E:(j + 1) * E],
                start=True, stop=True)],
                reads=[r_U, r_maskt[j]], writes=[slotR[sp]])
            sc = next_slot()
            P.mm_group([lambda t, j=j, sc=sc: t.matmul(
                slot_ap(sc, 0, 1, 0, 8), onescol[:], maskt[:, j * E:(j + 1) * E],
                start=True, stop=True)],
                reads=[r_onescol, r_maskt[j]], writes=[slotR[sc]])
            P.do("vector", lambda v, j=j: v.tensor_copy(
                out=offsall[0:1, j * E:(j + 1) * E], in_=wrow[0:1, 0:E]),
                reads=[r_wrow], writes=[r_offsall])
            P.do("vector", lambda v, sc=sc: v.tensor_tensor(
                out=wrow[0:1, 0:E], in0=slot_ap(sc, 0, 1, 0, 8), in1=wrow[0:1, 0:E],
                op=ALU.add), reads=[slotR[sc]], writes=[r_wrow])
            sb = next_slot()
            P.mm_group([lambda t, j=j, sb=sb: t.matmul(
                slot_ap(sb, 0, 128, 0, 8), ones_f[0:1, 0:128],
                offsall[0:1, j * E:(j + 1) * E], start=True, stop=True)],
                reads=[r_ones, r_offsall], writes=[slotR[sb]])
            P.do("vector", lambda v, sp=sp: v.tensor_copy(
                out=postile[:], in_=slot_ap(sp, 0, 128, 0, 8)),
                reads=[slotR[sp]], writes=[r_postile])
            P.do("vector", lambda v, sb=sb: v.scalar_tensor_tensor(
                out=tmpa[:], in0=slot_ap(sb, 0, 128, 0, 8), scalar=-1.0, in1=postile[:],
                op0=ALU.add, op1=ALU.add),
                reads=[slotR[sb], r_postile], writes=[r_tmpa])
            P.do("vector", lambda v: v.tensor_scalar(
                out=tmpa[:], in0=tmpa[:], scalar1=float(CAP), scalar2=None, op0=ALU.min),
                reads=[], writes=[r_tmpa])
            P.do("vector", lambda v, j=j: v.scalar_tensor_tensor(
                out=tmpb[:], in0=tmpa[:], scalar=-float(CAP), in1=maskt[:, j * E:(j + 1) * E],
                op0=ALU.add, op1=ALU.mult),
                reads=[r_tmpa, r_maskt[j]], writes=[r_tmpb])
            P.do("vector", lambda v, j=j: v.tensor_scalar(
                out=addri[:, j * E:(j + 1) * E], in0=tmpb[:], scalar1=float(CAP),
                scalar2=None, op0=ALU.add),
                reads=[r_tmpb], writes=[r_addr[j]])
        allz = [t for zz in ybuf_zero_tok for t in zz]
        for jj in range(HT):
            j = half * HT + jj
            jr = j % 2
            P.do("scalar", lambda s, j=j, jr=jr: s.dma_start(
                out=ysc[:, jr * 1024:(jr + 1) * 1024],
                in_=y1b_scr[j * 128:(j + 1) * 128, :]),
                reads=[r_y1bscr[j]], writes=[r_ysc[jr]], extra=allz, dma=True)
            for e in range(E):
                t = P.do("gpsimd", lambda g, e=e, j=j, jr=jr: g.indirect_dma_start(
                    out=ybuf_scr,
                    out_offset=IndirectOffsetOnAxis(addri[:, j * E + e:j * E + e + 1], 0),
                    in_=ysc[:, jr * 1024:(jr + 1) * 1024],
                    in_offset=None, element_offset=e * CAPP * C),
                    reads=[r_addr[j], r_ysc[jr]], dma=True)
                ybuf_ready[e].append(t)
    scat_done = sorted(set(t for tt in ybuf_ready for t in tt))
    for rr in r_yetm:
        rr.init_deps.extend(scat_done)
    rt_done = []
    for rr in r_rtg:
        if rr.w is not None:
            rt_done.append(rr.w)
        rt_done.extend(t for t in rr.readers if t is not None)
    for rr in r_Hg:
        rr.init_deps.extend(sorted(set(rt_done)))

    wbuf_idx = 0
    for e in range(E):
        yb = e % 2
        y0 = yb * CT * CAP
        ry = r_y1g[yb]
        if e + 3 < E:
            do_cast(e + 3)
        gb = wbuf_idx % 2; wbuf_idx += 1
        ub = wbuf_idx % 2; wbuf_idx += 1
        db = wbuf_idx % 2; wbuf_idx += 1
        Wg, Wu, Wd = (W1 if gb else W0), (W1 if ub else W0), (W1 if db else W0)
        rWg, rWu, rWd = r_W[gb], r_W[ub], r_W[db]

        for cc in range(CT):
            P.do("sync", lambda s, cc=cc, e=e, y0=y0: s.dma_start(
                out=y1g[:, y0 + cc * CAP:y0 + (cc + 1) * CAP],
                in_=ybuf_scr[e * CAPP:e * CAPP + CAP, cc * 128:(cc + 1) * 128],
                transpose=True),
                reads=[], writes=[ry[cc]], extra=ybuf_ready[e], dma=True)
        if e != 0:
            load_gateup(e, Wg, rWg, gate_scr, r_gscr[e])
            load_gateup(e, Wu, rWu, up_scr, r_uscr[e])

        for s_ in range(ST):
            m0 = rmx[:, 2 * s_:2 * s_ + 1]
            m1 = rmx[:, 2 * s_ + 1:2 * s_ + 2]
            si = next_slot()
            emits = [lambda t, cc=cc, s_=s_, si=si, y0=y0: t.matmul(
                slot_ap(si, 0, 128, 0, 8),
                y1g[:, y0 + cc * CAP + s_ * 128:y0 + cc * CAP + (s_ + 1) * 128],
                RwT[:, cc * E:(cc + 1) * E],
                start=(cc == 0), stop=(cc == CT - 1)) for cc in range(CT)]
            P.mm_group(emits, reads=ry + [r_RwT], writes=[slotR[si]])
            pr = probs[:, s_ * E:(s_ + 1) * E]
            rp = r_probs[s_]
            P.do("vector", lambda v, si=si, m0=m0: v.reduce_max(
                out=m0, in_=slot_ap(si, 0, 128, 0, 8), axis=AX.X, negate=True),
                reads=[slotR[si]], writes=[r_rmx[s_]])
            P.do("scalar", lambda s, pr=pr, si=si, m0=m0, m1=m1: s.activation(
                out=pr, in_=slot_ap(si, 0, 128, 0, 8), func=AF.Exp, bias=m0,
                accum_out=m1),
                reads=[slotR[si], r_rmx[s_]], writes=[rp, r_rmx[s_]])
            P.do("vector", lambda v, m1=m1: v.reciprocal(out=m1, in_=m1),
                 reads=[], writes=[r_rmx[s_]])
            P.do("vector", lambda v, pr=pr, m1=m1: v.tensor_scalar(
                out=pr, in0=pr, scalar1=m1, scalar2=None, op0=ALU.mult),
                reads=[r_rmx[s_]], writes=[rp])
            s1i = next_slot()
            P.mm_group([lambda t, s_=s_, e=e, s1i=s1i: t.matmul(
                slot_ap(s1i, 0, 1, 0, 128), probs[:, s_ * E + e:s_ * E + e + 1],
                ident_f[:], start=True, stop=True)],
                reads=[rp, r_idf], writes=[slotR[s1i]])
            P.do("vector", lambda v, s1i=s1i: v.tensor_copy(
                out=wrow[:], in_=slot_ap(s1i, 0, 1, 0, 128)),
                reads=[slotR[s1i]], writes=[r_wrow])
            s2i = next_slot()
            P.mm_group([lambda t, s2i=s2i: t.matmul(
                slot_ap(s2i, 0, 128, 0, 128), ones_f[0:1, 0:128],
                wrow[:], start=True, stop=True)],
                reads=[r_wrow, r_ones], writes=[slotR[s2i]])
            P.do("vector", lambda v, s_=s_, s2i=s2i: v.tensor_copy(
                out=bc_sb[:, s_ * 128:(s_ + 1) * 128],
                in_=slot_ap(s2i, 0, 128, 0, 128)),
                reads=[slotR[s2i]], writes=[r_bc])

        for ii in range(IT):
            pi = ii % 2
            prA, prB = pairR(pi)
            emits = []
            pre = []
            for (tco, tcl) in TCS:
                for cc in range(CT):
                    emits.append(lambda t, ii=ii, tco=tco, tcl=tcl, cc=cc, pi=pi, Wg=Wg, y0=y0: t.matmul(
                        pair_ap(pi, tco, tco + tcl),
                        Wg[:, cc * I + ii * 128:cc * I + (ii + 1) * 128],
                        y1g[:, y0 + cc * CAP + tco:y0 + cc * CAP + tco + tcl],
                        start=(cc == 0), stop=(cc == CT - 1)))
                    pre.append([rWg[cc], ry[cc]])
            P.mm_group(emits, writes=[prA, prB], per_reads=pre, reg_reads=rWg[CT:])
            P.do("scalar", lambda s, ii=ii, pi=pi: s.activation(
                out=Hg[:, ii * CAP:(ii + 1) * CAP], in_=pair_ap(pi, 0, CAP),
                func=AF.Silu),
                reads=[prA, prB], writes=[r_Hg[ii]])

        for ii in range(IT):
            P.do("scalar", lambda s, ii=ii, e=e, Wd=Wd: s.dma_start(
                out=Wd[:, ii * 1024:(ii + 1) * 1024],
                in_=down_scr[e * C:(e + 1) * C, ii * 128:(ii + 1) * 128],
                transpose=True),
                reads=[r_dscr[e]], writes=[rWd[ii]], dma=True)

        for ii in range(IT):
            pi = 2 + ii % 2
            prA, prB = pairR(pi)
            emits = []
            pre = []
            for (tco, tcl) in TCS:
                for cc in range(CT):
                    emits.append(lambda t, ii=ii, tco=tco, tcl=tcl, cc=cc, pi=pi, Wu=Wu, y0=y0: t.matmul(
                        pair_ap(pi, tco, tco + tcl),
                        Wu[:, cc * I + ii * 128:cc * I + (ii + 1) * 128],
                        y1g[:, y0 + cc * CAP + tco:y0 + cc * CAP + tco + tcl],
                        start=(cc == 0), stop=(cc == CT - 1)))
                    pre.append([rWu[cc], ry[cc]])
            P.mm_group(emits, writes=[prA, prB], per_reads=pre, reg_reads=rWu[CT:])
            hs = ii % 2
            P.do("vector", lambda v, hs=hs, pi=pi: v.tensor_tensor(
                out=Hu[:, hs * CAP:(hs + 1) * CAP], in0=pair_ap(pi, 0, CAP),
                in1=bc_sb[:, 0:CAP], op=ALU.mult),
                reads=[prA, prB, r_bc], writes=[r_Hu[hs]])
            heng = "gpsimd" if ii % 2 == 0 else "vector"
            P.do(heng, lambda g, ii=ii, hs=hs: g.tensor_tensor(
                out=Hg[:, ii * CAP:(ii + 1) * CAP],
                in0=Hg[:, ii * CAP:(ii + 1) * CAP],
                in1=Hu[:, hs * CAP:(hs + 1) * CAP], op=ALU.mult),
                reads=[r_Hu[hs]], writes=[r_Hg[ii]])

        for h in range(CT):
            for tci, (tco, tcl) in enumerate(TCS):
                si = next_slot()
                emits = [lambda t, ii=ii, h=h, tco=tco, tcl=tcl, si=si, Wd=Wd: t.matmul(
                    slot_ap(si, 0, 128, 0, tcl),
                    Wd[:, ii * 1024 + h * 128:ii * 1024 + (h + 1) * 128],
                    Hg[:, ii * CAP + tco:ii * CAP + tco + tcl],
                    start=(ii == 0), stop=(ii == IT - 1)) for ii in range(IT)]
                pre = [[rWd[ii], r_Hg[ii]] for ii in range(IT)]
                P.mm_group(emits, writes=[slotR[si]], per_reads=pre)
                P.do("vector", lambda v, h=h, tco=tco, tcl=tcl, si=si: v.tensor_copy(
                    out=Ye_sb[:, h * CAP + tco:h * CAP + tco + tcl],
                    in_=slot_ap(si, 0, 128, 0, tcl)),
                    reads=[slotR[si]], writes=[r_Yesb[h][tci]])

        for s_ in range(ST):
            sr = s_ % 2
            yreads = [r_Yesb[ccf][0 if s_ * 128 < 512 else 1] for ccf in range(CT)]
            for ccf in range(CT):
                si = next_slot()
                P.mm_group([lambda t, ccf=ccf, s_=s_, si=si: t.transpose(
                    slot_ap(si, 0, 128, 0, 128),
                    Ye_sb[:, ccf * CAP + s_ * 128:ccf * CAP + (s_ + 1) * 128],
                    ident_f[:])],
                    reads=[yreads[ccf], r_idf], writes=[slotR[si]])
                P.do("scalar", lambda s, ccf=ccf, sr=sr, si=si: s.copy(
                    out=yetm[:, sr * 1024 + ccf * 128:sr * 1024 + (ccf + 1) * 128],
                    in_=slot_ap(si, 0, 128, 0, 128)),
                    reads=[slotR[si]], writes=[r_yetm[sr]])
            P.do("scalar", lambda s, e=e, s_=s_, sr=sr: s.dma_start(
                out=ye_scr[e * CAPP + s_ * 128:e * CAPP + (s_ + 1) * 128, :],
                in_=yetm[:, sr * 1024:(sr + 1) * 1024]),
                reads=[r_yetm[sr]], writes=[r_Yescr[e]], dma=True)

    for j in range(NT):
        sl = j % 2
        lt = ln2t[:, sl * 1024:(sl + 1) * 1024]
        rlt = r_ln2t[sl]
        for e in range(E):
            P.do("gpsimd", lambda g, e=e, j=j, lt=lt: g.indirect_dma_start(
                out=lt, out_offset=None,
                in_=ye_scr,
                in_offset=IndirectOffsetOnAxis(addri[:, j * E + e:j * E + e + 1], 0),
                element_offset=e * CAPP * C,
                compute_op=(ALU.bypass if e == 0 else ALU.add)),
                reads=[r_addr[j], r_Yescr[e]], writes=[rlt], dma=True)
        P.do("scalar", lambda s, j=j: s.dma_start(
            out=y1r[:], in_=y1f_scr[j * 128:(j + 1) * 128, :]),
            reads=[r_y1fscr[j]], writes=[r_y1r], dma=True)
        P.do("vector", lambda v, lt=lt: v.tensor_tensor(
            out=lt, in0=lt, in1=y1r[:], op=ALU.add),
            reads=[r_y1r], writes=[rlt])
        P.do("vector", lambda v, j=j, lt=lt: v.reduce_sum(
            out=sumc[:, j:j + 1], in_=lt, axis=AX.X),
            reads=[rlt], writes=[r_sumc[j]])
        P.do("scalar", lambda s, j=j, lt=lt: s.activation(
            out=sqd[:], in_=lt, func=AF.Square, accum_out=sqc[:, j:j + 1]),
            reads=[rlt], writes=[r_sqd, r_sqc[j]])
        P.do("vector", lambda v, j=j: v.tensor_scalar(
            out=meanc[:, j:j + 1], in0=sumc[:, j:j + 1], scalar1=1.0 / C,
            scalar2=None, op0=ALU.mult), reads=[r_sumc[j]], writes=[r_meanc[j]])
        P.do("vector", lambda v, j=j: v.tensor_tensor(
            out=tmpc[:, j:j + 1], in0=meanc[:, j:j + 1], in1=meanc[:, j:j + 1],
            op=ALU.mult), reads=[r_meanc[j]], writes=[r_tmpc[j]])
        P.do("vector", lambda v, j=j: v.scalar_tensor_tensor(
            out=varc[:, j:j + 1], in0=sqc[:, j:j + 1], scalar=1.0 / C,
            in1=tmpc[:, j:j + 1], op0=ALU.mult, op1=ALU.subtract),
            reads=[r_sqc[j], r_tmpc[j]], writes=[r_varc[j]])
        P.do("scalar", lambda s, j=j: s.activation(
            out=stdc[:, j:j + 1], in_=varc[:, j:j + 1], func=AF.Sqrt, bias=epsc[:]),
            reads=[r_varc[j], r_eps], writes=[r_stdc[j]])
        P.do("vector", lambda v, j=j: v.reciprocal(
            out=rstdc[:, j:j + 1], in_=stdc[:, j:j + 1]),
            reads=[r_stdc[j]], writes=[r_rstdc[j]])
        P.do("vector", lambda v, j=j, lt=lt: v.tensor_scalar(
            out=lt, in0=lt, scalar1=meanc[:, j:j + 1],
            scalar2=rstdc[:, j:j + 1], op0=ALU.subtract, op1=ALU.mult),
            reads=[r_meanc[j], r_rstdc[j]], writes=[rlt])
        P.do("scalar", lambda s, j=j, lt=lt: s.dma_start(
            out=out_d[j * 128:(j + 1) * 128, :], in_=lt),
            reads=[rlt], dma=True)

    final = P.counts["scalar"]
    P._op("scalar", lambda s: s.nop(), [("scalar", final)], sig=False)
    P.emit()
    return nc


def kernel(L0_attn_x, L0_attn_w_qkv, L0_attn_w_conv1, L0_ffn_router_w,
           L0_ffn_gate_w, L0_ffn_up_w, L0_ffn_down_w, _upto="full"):
    x = np.ascontiguousarray(np.asarray(L0_attn_x, dtype=np.float32))
    wqkv = np.ascontiguousarray(np.asarray(L0_attn_w_qkv, dtype=np.float32))
    wconv = np.ascontiguousarray(np.asarray(L0_attn_w_conv1, dtype=np.float32)).reshape(1, C)
    router = np.ascontiguousarray(np.asarray(L0_ffn_router_w, dtype=np.float32))
    gate = np.ascontiguousarray(np.asarray(L0_ffn_gate_w, dtype=np.float32)).reshape(E * I, C)
    up = np.ascontiguousarray(np.asarray(L0_ffn_up_w, dtype=np.float32)).reshape(E * I, C)
    down = np.ascontiguousarray(np.asarray(L0_ffn_down_w, dtype=np.float32)).reshape(E * C, I)

    nc = build(_upto)
    in_maps = []
    for b in range(8):
        in_maps.append({
            "x": np.ascontiguousarray(x[b]),
            "wqkv": wqkv, "wconv": wconv, "router": router,
            "gate": gate, "up": up, "down": down,
        })
    res = run_bass_kernel_spmd(nc, in_maps, core_ids=list(range(8)))
    return np.stack([res.results[b]["out"] for b in range(8)], axis=0)


# version 15 (final)
# speedup vs baseline: 1.4602x; 1.3393x over previous
import numpy as np
import concourse.bass as bass
import concourse.mybir as mybir
from concourse.bass_core_rust import IndirectOffsetOnAxis
from concourse.bass_utils import run_bass_kernel_spmd

F32 = mybir.dt.float32
BF16 = mybir.dt.bfloat16
AF = mybir.ActivationFunctionType
ALU = mybir.AluOpType
AX = mybir.AxisListType

N, C = 2048, 1024
NT = 16
CT = 8
E, I = 8, 2816
IT = 22
NH = 2
HN = N // NH
HT = NT // NH
CAP = 768
CAPP = 1024
ST = CAP // 128
TCS = [(0, 512), (512, CAP - 512)]
SCALE = float(C) ** -0.5
EPS = 1e-5

ENGS = ("sync", "tensor", "vector", "scalar", "gpsimd")


class R:
    __slots__ = ("w", "readers", "init_deps", "name")

    def __init__(self, name="r"):
        self.w = None
        self.readers = []
        self.init_deps = []
        self.name = name


class Prog:
    def __init__(self, nc):
        self.nc = nc
        self.ops = {e: [] for e in ENGS}
        self.sems = {e: nc.alloc_semaphore("sem_" + e) for e in ENGS}
        self.counts = {e: 0 for e in ENGS}
        self.last_wait = {e: {} for e in ENGS}

    def _op(self, eng, emit, toks, sig=True, dma=False):
        lw = self.last_wait[eng]
        waits = []
        for (s, v) in sorted(set(toks)):
            if s == eng and v > self.counts[eng]:
                raise AssertionError(f"future self-wait {s} {v}")
            if lw.get(s, 0) < v:
                waits.append((s, v))
                lw[s] = v
        inc = 16 if dma else (1 if sig else 0)
        tok = None
        if inc:
            self.counts[eng] += inc
            tok = (eng, self.counts[eng])
        self.ops[eng].append((waits, emit, self.sems[eng] if inc else None, inc))
        return tok

    def _gather(self, reads, writes, extra):
        toks = [t for t in extra if t is not None]
        for r in reads:
            if r.w is not None:
                toks.append(r.w)
            toks.extend(r.init_deps)
            r.init_deps = []
        for r in writes:
            if r.w is not None:
                toks.append(r.w)
            toks.extend(t for t in r.readers if t is not None)
            toks.extend(r.init_deps)
            r.init_deps = []
        return toks

    def do(self, eng, emit, reads=(), writes=(), extra=(), dma=False):
        toks = self._gather(reads, writes, extra)
        tok = self._op(eng, emit, toks, sig=True, dma=dma)
        for r in reads:
            r.readers.append(tok)
        for r in writes:
            r.w = tok
            r.readers = []
        return tok

    def mm_group(self, emits, reads=(), writes=(), extra=(), per_reads=None,
                 reg_reads=()):
        toks = self._gather(reads, writes, extra)
        n = len(emits)
        tok = None
        allr = list(reads)
        for i, e in enumerate(emits):
            ti = list(toks) if i == 0 else []
            if per_reads is not None:
                for r in per_reads[i]:
                    if r.w is not None:
                        ti.append(r.w)
                    ti.extend(r.init_deps)
                    r.init_deps = []
                    allr.append(r)
            t = self._op("tensor", e, ti, sig=(i == n - 1))
            if t is not None:
                tok = t
        allr.extend(reg_reads)
        for r in allr:
            r.readers.append(tok)
        for r in writes:
            r.w = tok
            r.readers = []
        return tok

    def emit(self):
        nc = self.nc
        with nc.Block() as block:
            for eng_name in ENGS:
                sems = self.sems

                def make(oplist):
                    def run(eng):
                        for waits, emit, inc_sem, inc_amt in oplist:
                            for (s, v) in waits:
                                eng.wait_ge(sems[s], v)
                            inst = emit(eng)
                            if inc_sem is not None:
                                inst.then_inc(inc_sem, inc_amt)
                    return run

                getattr(block, eng_name)(make(self.ops[eng_name]))


def build(upto="full", n_experts=E):
    nc = bass.Bass("TRN2", target_bir_lowering=False)

    x_d = nc.dram_tensor("x", [N, C], F32, kind="ExternalInput").ap()
    wqkv_d = nc.dram_tensor("wqkv", [3 * C, C], F32, kind="ExternalInput").ap()
    wconv_d = nc.dram_tensor("wconv", [1, C], F32, kind="ExternalInput").ap()
    router_d = nc.dram_tensor("router", [E, C], F32, kind="ExternalInput").ap()
    gate_d = nc.dram_tensor("gate", [E * I, C], F32, kind="ExternalInput").ap()
    up_d = nc.dram_tensor("up", [E * I, C], F32, kind="ExternalInput").ap()
    down_d = nc.dram_tensor("down", [E * C, I], F32, kind="ExternalInput").ap()
    out_d = nc.dram_tensor("out", [N, C], F32, kind="ExternalOutput").ap()

    gate_scr = nc.dram_tensor("gate_scr", [E * I, C], BF16, kind="Internal").ap()
    up_scr = nc.dram_tensor("up_scr", [E * I, C], BF16, kind="Internal").ap()
    down_scr = nc.dram_tensor("down_scr", [E * C, I], BF16, kind="Internal").ap()
    y1b_scr = nc.dram_tensor("y1b_scr", [N, C], BF16, kind="Internal").ap()
    y1f_scr = nc.dram_tensor("y1f_scr", [N, C], F32, kind="Internal").ap()
    ybuf_scr = nc.dram_tensor("ybuf_scr", [E * CAPP, C], BF16, kind="Internal").ap()
    ye_scr = nc.dram_tensor("ye_scr", [E * CAPP, C], F32, kind="Internal").ap()

    ident_f = nc.alloc_sbuf_tensor("ident_f", [128, 128], F32)
    ones_f = nc.alloc_sbuf_tensor("ones_f", [1, 128], F32)
    wrow = nc.alloc_sbuf_tensor("wrow", [1, 128], F32)
    bc_sb = nc.alloc_sbuf_tensor("bc_sb", [128, 1024], F32)
    sqd = nc.alloc_sbuf_tensor("sqd", [128, 1024], F32)
    sumc = nc.alloc_sbuf_tensor("sumc", [128, NT], F32)
    sqc = nc.alloc_sbuf_tensor("sqc", [128, NT], F32)
    meanc = nc.alloc_sbuf_tensor("meanc", [128, NT], F32)
    rstdc = nc.alloc_sbuf_tensor("rstdc", [128, NT], F32)
    tmpc = nc.alloc_sbuf_tensor("tmpc", [128, NT], F32)
    varc = nc.alloc_sbuf_tensor("varc", [128, NT], F32)
    stdc = nc.alloc_sbuf_tensor("stdc", [128, NT], F32)
    agc = nc.alloc_sbuf_tensor("agc", [128, NT], F32)
    s2c = nc.alloc_sbuf_tensor("s2c", [128, NT], F32)
    w2c = nc.alloc_sbuf_tensor("w2c", [128, NT], F32)
    mxc = nc.alloc_sbuf_tensor("mxc", [128, 4], F32)
    epsc = nc.alloc_sbuf_tensor("epsc", [128, 1], F32)
    cols_bf = nc.alloc_sbuf_tensor("cols_bf", [128, 8], BF16)
    W0 = nc.alloc_sbuf_tensor("W0", [128, IT * 1024], BF16)
    W1 = nc.alloc_sbuf_tensor("W1", [128, IT * 1024], BF16)
    RwT = nc.alloc_sbuf_tensor("RwT", [128, CT * E], BF16)

    rem = nc.sbuf_bytes_remaining
    top = 229344
    base = top - rem

    def at(name, shape, dtype, off):
        return nc.alloc_sbuf_tensor_at(name, shape, dtype, offset=base + off)

    x_tm = at("x_tm", [128, NT * 1024], F32, 0)
    qkv_buf = at("qkv_buf", [128, CT * 1024], BF16, 65536)
    y1b_st = at("y1b_st", [128, 2 * 1024], BF16, 65536 + 16384)
    rows = at("rows", [1, 2048], F32, 65536 + 16384 + 4096)
    wstg = at("wstg", [128, 2048], F32, 65536 + 16384 + 4096 + 8192)
    attn_end = 65536 + 16384 + 4096 + 8192 + 8192
    Hg = at("Hg", [128, IT * CAP], BF16, 0)
    o = IT * CAP * 2
    y1g = at("y1g", [128, 2 * CT * CAP], BF16, o); o += 2 * CT * CAP * 2
    Ye_sb = at("Ye_sb", [128, CT * CAP], F32, o); o += CT * CAP * 4
    Hu = at("Hu", [128, 2 * CAP], BF16, o); o += 2 * CAP * 2
    ln2t = at("ln2t", [128, 2048], F32, o); o += 8192
    y1r = at("y1r", [128, 1024], F32, o); o += 4096
    yetm = at("yetm", [128, 2048], F32, o)
    ysc = at("ysc", [128, 2048], BF16, o)
    o += 8192
    zero_t = at("zero_t", [128, 1024], BF16, o); o += 2048
    probs = at("probs", [128, HT * E], F32, o); o += 256
    wte = at("wte", [128, NT * E], F32, o); o += 512
    ge1 = at("ge1", [128, 2 * E], F32, o); o += 64
    addri = at("addri", [128, NT * E], mybir.dt.int32, o); o += 512
    U128 = at("U128", [128, 128], F32, o); o += 512
    maskt = at("maskt", [128, NT * E], F32, o); o += 512
    postile = at("postile", [128, E], F32, o); o += 64
    offsall = at("offsall", [1, 128], F32, o); o += 512
    onescol = at("onescol", [128, 1], F32, o); o += 32
    tmpa = at("tmpa", [128, E], F32, o); o += 64
    tmpb = at("tmpb", [128, E], F32, o); o += 64
    rmx = at("rmx", [128, 2 * NT], F32, o); o += 128
    m1t = at("m1t", [128, E], F32, o); o += 64
    eoff = at("eoff", [128, E], F32, o); o += 64
    ffn_end = o
    assert base + max(ffn_end, attn_end) <= top - 4, (base, ffn_end)

    psT = [nc.alloc_psum_tensor(f"ps{i}", [128, 1024], F32) for i in range(4)]

    P = Prog(nc)

    slot_t = [(psT[i // 2], (i % 2) * 512) for i in range(8)]
    slotR = [R(f"ps{i}") for i in range(8)]

    def slot_ap(i, p0, pn, f0, fn):
        t, off = slot_t[i]
        return t[p0:p0 + pn, off + f0:off + fn]

    def pair_ap(pi, f0, fn):
        return psT[pi][:, f0:fn]

    def pairR(pi):
        return (slotR[2 * pi], slotR[2 * pi + 1])

    xt = [R(f"x{j}") for j in range(NT)]
    r_qkv = [R(f"qkv{c}") for c in range(CT)]
    r_rows = [R("rows0"), R("rows1")]
    r_wstg = [R("wstg0"), R("wstg1")]
    r_wrow = R("wrow")
    r_bc = R("bc")
    r_sqd = R("sqd")
    r_agc, r_s2c, r_w2c = R("agc"), R("s2c"), R("w2c")
    r_mxc = R("mxc")
    r_colsbf = R("cols_bf")
    r_sumc = [R(f"sumc{j}") for j in range(NT)]
    r_sqc = [R(f"sqc{j}") for j in range(NT)]
    r_meanc = [R(f"meanc{j}") for j in range(NT)]
    r_rstdc = [R(f"rstdc{j}") for j in range(NT)]
    r_tmpc = [R(f"tmpc{j}") for j in range(NT)]
    r_varc = [R(f"varc{j}") for j in range(NT)]
    r_stdc = [R(f"stdc{j}") for j in range(NT)]
    r_y1bst = [R("y1b_st0"), R("y1b_st1")]
    r_idf, r_ones = R("ident_f"), R("ones_f")
    r_eps = R("epsc")
    r_W = [[R(f"W0_{i}") for i in range(IT)], [R(f"W1_{i}") for i in range(IT)]]
    r_RwT = R("RwT")
    r_gscr = [R(f"gscr{e}") for e in range(E)]
    r_uscr = [R(f"uscr{e}") for e in range(E)]
    r_dscr = [R(f"dscr{e}") for e in range(E)]
    r_y1bscr = [R(f"y1bscr{j}") for j in range(NT)]
    r_y1fscr = [R(f"y1fscr{j}") for j in range(NT)]
    r_Hg = [R(f"Hg{i}") for i in range(IT)]
    r_rtg = [R(f"rtg{c}") for c in range(CT)]
    r_y1g = [[R(f"y1g{b}_{c}") for c in range(CT)] for b in range(2)]
    r_rmx = [R(f"rmx{j}") for j in range(NT)]
    r_Yesb = [[R(f"Ye{h}_{t}") for t in range(len(TCS))] for h in range(CT)]
    r_Hu = [R("Hu0"), R("Hu1")]
    r_ln2t = [R("ln2t0"), R("ln2t1")]
    r_y1r = R("y1r")
    r_yetm = [R("yetm0"), R("yetm1")]
    r_ysc = [R("ysc0"), R("ysc1")]
    r_zero = R("zero_t")
    r_probs = [R(f"probs{j}") for j in range(HT)]
    r_wte = [R(f"wte{j}") for j in range(NT)]
    r_ge1 = [R("ge1a"), R("ge1b")]
    r_maskt = [R(f"maskt{j}") for j in range(NT)]
    r_addr = [R(f"addr{j}") for j in range(NT)]
    r_U, r_onescol, r_offsall = R("U128"), R("onescol"), R("offsall")
    r_postile, r_tmpa, r_tmpb = R("postile"), R("tmpa"), R("tmpb")
    r_m1t, r_eoff = R("m1t"), R("eoff")
    r_Yescr = [R(f"Yescr{e}") for e in range(E)]

    P.do("gpsimd", lambda g: g.memset(ident_f[:], 0.0), writes=[r_idf])
    P.do("gpsimd", lambda g: g.affine_select(
        out=ident_f[:], in_=ident_f[:], compare_op=ALU.not_equal, fill=1.0,
        base=0, pattern=[[-1, 128]], channel_multiplier=1),
        writes=[r_idf])
    P.do("vector", lambda v: v.memset(ones_f[:], 1.0), writes=[r_ones])
    P.do("vector", lambda v: v.memset(epsc[:], EPS), writes=[r_eps])

    def do_cast(e, extra=()):
        P.do("gpsimd", lambda g, e=e: g.dma_start(
            out=gate_scr[e * I:(e + 1) * I, :], in_=gate_d[e * I:(e + 1) * I, :]),
            writes=[r_gscr[e]], extra=extra, dma=True)
        P.do("gpsimd", lambda g, e=e: g.dma_start(
            out=up_scr[e * I:(e + 1) * I, :], in_=up_d[e * I:(e + 1) * I, :]),
            writes=[r_uscr[e]], extra=extra, dma=True)
        P.do("gpsimd", lambda g, e=e: g.dma_start(
            out=down_scr[e * C:(e + 1) * C, :], in_=down_d[e * C:(e + 1) * C, :]),
            writes=[r_dscr[e]], extra=extra, dma=True)

    def load_gateup(e, W, rW, scr, rscr):
        for cc in range(CT):
            P.do("sync", lambda s, cc=cc, e=e, W=W: s.dma_start(
                out=W[:, cc * I:(cc + 1) * I],
                in_=scr[e * I:(e + 1) * I, cc * 128:(cc + 1) * 128],
                transpose=True),
                reads=[rscr], writes=[rW[cc]], dma=True)

    tok_x = P.do("scalar", lambda s: s.dma_start(
        out=x_tm[:].rearrange("p (j c) -> p j c", c=1024),
        in_=x_d.rearrange("(j p) c -> p j c", p=128)), dma=True)
    for j in range(NT):
        xt[j].w = tok_x
    P.do("scalar", lambda s: s.dma_start(out=rows[0:1, 0:1024], in_=wconv_d),
         writes=[r_rows[0]], dma=True)

    SL = [0]

    def next_slot():
        i = SL[0] % 8
        SL[0] += 1
        return i

    def next_pair():
        i0 = (SL[0] % 4) * 2
        SL[0] += 2
        return i0

    def row_to_cols_bf(row_slot):
        for c in range(CT):
            si = next_slot()
            P.mm_group([lambda t, c=c, si=si, rs=row_slot: t.matmul(
                slot_ap(si, 0, 128, 0, 1),
                rows[0:1, rs * 1024 + c * 128:rs * 1024 + (c + 1) * 128],
                ones_f[0:1, 0:1], start=True, stop=True)],
                reads=[r_rows[row_slot], r_ones], writes=[slotR[si]])
            P.do("scalar", lambda s, c=c, si=si: s.copy(
                out=cols_bf[:, c:c + 1], in_=slot_ap(si, 0, 128, 0, 1)),
                reads=[slotR[si]], writes=[r_colsbf])

    def broadcast_row_to_bc(row_slot):
        i0 = next_pair()
        for h in range(2):
            P.mm_group([lambda t, h=h, i0=i0, rs=row_slot: t.matmul(
                slot_ap(i0 + h, 0, 128, 0, 512),
                ones_f[0:1, 0:128],
                rows[0:1, rs * 1024 + h * 512:rs * 1024 + (h + 1) * 512],
                start=True, stop=True)],
                reads=[r_rows[row_slot], r_ones], writes=[slotR[i0 + h]])
        P.do("vector", lambda v, i0=i0: v.tensor_copy(
            out=bc_sb[:], in_=pair_ap(i0 // 2, 0, 1024)),
            reads=[slotR[i0], slotR[i0 + 1]], writes=[r_bc])

    def matvec_row(lhs_cols_res, lhs_col_ap, rhs_buf_ap, rhs_res, out_row_slot,
                   nk=CT, post_scale=None):
        i0 = next_pair()
        for h in range(2):
            emits = [lambda t, k=k, h=h, i0=i0: t.matmul(
                slot_ap(i0 + h, 0, 1, 0, 512),
                lhs_col_ap(k),
                rhs_buf_ap(k, h),
                start=(k == 0), stop=(k == nk - 1)) for k in range(nk)]
            P.mm_group(emits, reads=list(lhs_cols_res) + list(rhs_res),
                       writes=[slotR[i0 + h]])
        if post_scale is None:
            P.do("vector", lambda v, i0=i0, rs=out_row_slot: v.tensor_copy(
                out=rows[0:1, rs * 1024:(rs + 1) * 1024], in_=pair_ap(i0 // 2, 0, 1024)[0:1, :]),
                reads=[slotR[i0], slotR[i0 + 1]], writes=[r_rows[out_row_slot]])
        else:
            P.do("vector", lambda v, i0=i0, rs=out_row_slot, sc=post_scale: v.tensor_scalar(
                out=rows[0:1, rs * 1024:(rs + 1) * 1024], in0=pair_ap(i0 // 2, 0, 1024)[0:1, :],
                scalar1=sc, scalar2=None, op0=ALU.mult),
                reads=[slotR[i0], slotR[i0 + 1]], writes=[r_rows[out_row_slot]])

    def load_nat_bf(row0):
        for r in range(CT):
            pp = r % 2
            P.do("scalar", lambda s, r=r, pp=pp: s.dma_start(
                out=wstg[:, pp * 1024:(pp + 1) * 1024],
                in_=wqkv_d[row0 + r * 128:row0 + (r + 1) * 128, :]),
                writes=[r_wstg[pp]], dma=True)
            P.do("scalar", lambda s, r=r, pp=pp: s.copy(
                out=qkv_buf[:, r * 1024:(r + 1) * 1024],
                in_=wstg[:, pp * 1024:(pp + 1) * 1024]),
                reads=[r_wstg[pp]], writes=[r_qkv[r]])

    def load_T_bf(row0):
        for r in range(CT):
            pp = r % 2
            P.do("scalar", lambda s, r=r, pp=pp: s.dma_start(
                out=wstg[:, pp * 1024:(pp + 1) * 1024],
                in_=wqkv_d[row0 + r * 128:row0 + (r + 1) * 128, :]),
                writes=[r_wstg[pp]], dma=True)
            for t in range(CT):
                si = next_slot()
                P.mm_group([lambda tn, r=r, t=t, pp=pp, si=si: tn.transpose(
                    slot_ap(si, 0, 128, 0, 128),
                    wstg[:, pp * 1024 + t * 128:pp * 1024 + (t + 1) * 128],
                    ident_f[:])],
                    reads=[r_wstg[pp], r_idf], writes=[slotR[si]])
                P.do("scalar", lambda s, r=r, t=t, si=si: s.copy(
                    out=qkv_buf[:, t * 1024 + r * 128:t * 1024 + (r + 1) * 128],
                    in_=slot_ap(si, 0, 128, 0, 128)),
                    reads=[slotR[si]], writes=[r_qkv[t]])

    load_nat_bf(0)
    row_to_cols_bf(0)
    matvec_row([r_colsbf], lambda k: cols_bf[:, k:k + 1],
               lambda k, h: qkv_buf[:, k * 1024 + h * 512:k * 1024 + (h + 1) * 512],
               r_qkv, 1)
    broadcast_row_to_bc(1)
    for j in range(NT):
        P.do("vector", lambda v, j=j: v.scalar_tensor_tensor(
            out=sqd[:], in0=x_tm[:, j * 1024:(j + 1) * 1024], scalar=1.0,
            in1=bc_sb[:], op0=ALU.mult, op1=ALU.mult,
            accum_out=agc[:, j:j + 1]),
            reads=[xt[j], r_bc], writes=[r_sqd, r_agc])
    matvec_row([r_agc] + xt, lambda k: agc[:, k:k + 1],
               lambda k, h: x_tm[:, k * 1024 + h * 512:k * 1024 + (h + 1) * 512],
               [], 0, nk=NT, post_scale=SCALE)
    row_to_cols_bf(0)
    load_T_bf(C)
    do_cast(0, extra=[("scalar", P.counts["scalar"])])
    load_gateup(0, W0, r_W[0], gate_scr, r_gscr[0])
    matvec_row([r_colsbf], lambda k: cols_bf[:, k:k + 1],
               lambda k, h: qkv_buf[:, k * 1024 + h * 512:k * 1024 + (h + 1) * 512],
               r_qkv, 0)
    P.do("vector", lambda v: v.reduce_max(out=mxc[0:1, 0:1], in_=rows[0:1, 0:1024],
                                          axis=AX.X, negate=True),
         reads=[r_rows[0]], writes=[r_mxc])
    P.do("scalar", lambda s: s.activation(out=rows[0:1, 0:1024], in_=rows[0:1, 0:1024],
                                          func=AF.Exp, bias=mxc[0:1, 0:1]),
         reads=[r_mxc], writes=[r_rows[0]])
    P.do("vector", lambda v: v.reduce_sum(out=mxc[0:1, 1:2], in_=rows[0:1, 0:1024], axis=AX.X),
         reads=[r_rows[0]], writes=[r_mxc])
    P.do("vector", lambda v: v.reciprocal(out=mxc[0:1, 2:3], in_=mxc[0:1, 1:2]),
         reads=[r_mxc], writes=[r_mxc])
    P.do("vector", lambda v: v.tensor_scalar(out=rows[0:1, 0:1024], in0=rows[0:1, 0:1024],
                                             scalar1=mxc[0:1, 2:3], scalar2=None, op0=ALU.mult),
         reads=[r_mxc], writes=[r_rows[0]])
    row_to_cols_bf(0)
    load_nat_bf(2 * C)
    do_cast(1, extra=[("scalar", P.counts["scalar"])])
    load_gateup(0, W1, r_W[1], up_scr, r_uscr[0])
    matvec_row([r_colsbf], lambda k: cols_bf[:, k:k + 1],
               lambda k, h: qkv_buf[:, k * 1024 + h * 512:k * 1024 + (h + 1) * 512],
               r_qkv, 1)
    broadcast_row_to_bc(1)
    for j in range(NT):
        P.do("vector", lambda v, j=j: v.scalar_tensor_tensor(
            out=sqd[:], in0=x_tm[:, j * 1024:(j + 1) * 1024], scalar=1.0,
            in1=bc_sb[:], op0=ALU.mult, op1=ALU.mult,
            accum_out=s2c[:, j:j + 1]),
            reads=[xt[j], r_bc], writes=[r_sqd, r_s2c])
    P.do("vector", lambda v: v.reduce_max(out=mxc[:, 0:1], in_=s2c[:], axis=AX.X),
         reads=[r_s2c], writes=[r_mxc])
    si = next_slot()
    P.mm_group([lambda t, si=si: t.matmul(
        slot_ap(si, 0, 1, 0, 128), mxc[:, 0:1], ident_f[:], start=True, stop=True)],
        reads=[r_mxc, r_idf], writes=[slotR[si]])
    P.do("vector", lambda v, si=si: v.tensor_copy(out=wrow[:], in_=slot_ap(si, 0, 1, 0, 128)),
         reads=[slotR[si]], writes=[r_wrow])
    P.do("vector", lambda v: v.reduce_max(out=mxc[0:1, 1:2], in_=wrow[:],
                                          axis=AX.X, negate=True),
         reads=[r_wrow], writes=[r_mxc])
    si = next_slot()
    P.mm_group([lambda t, si=si: t.matmul(
        slot_ap(si, 0, 128, 0, 1), ones_f[0:1, 0:128], mxc[0:1, 1:2], start=True, stop=True)],
        reads=[r_mxc, r_ones], writes=[slotR[si]])
    P.do("vector", lambda v, si=si: v.tensor_copy(out=mxc[:, 2:3], in_=slot_ap(si, 0, 128, 0, 1)),
         reads=[slotR[si]], writes=[r_mxc])
    P.do("scalar", lambda s: s.activation(out=w2c[:], in_=s2c[:], func=AF.Exp,
                                          bias=mxc[:, 2:3]),
         reads=[r_s2c, r_mxc], writes=[r_w2c])
    P.do("vector", lambda v: v.reduce_sum(out=mxc[:, 0:1], in_=w2c[:], axis=AX.X),
         reads=[r_w2c], writes=[r_mxc])
    si = next_slot()
    P.mm_group([lambda t, si=si: t.matmul(
        slot_ap(si, 0, 1, 0, 128), mxc[:, 0:1], ident_f[:], start=True, stop=True)],
        reads=[r_mxc, r_idf], writes=[slotR[si]])
    P.do("vector", lambda v, si=si: v.tensor_copy(out=wrow[:], in_=slot_ap(si, 0, 1, 0, 128)),
         reads=[slotR[si]], writes=[r_wrow])
    P.do("vector", lambda v: v.reduce_sum(out=mxc[0:1, 1:2], in_=wrow[:], axis=AX.X),
         reads=[r_wrow], writes=[r_mxc])
    P.do("vector", lambda v: v.reciprocal(out=mxc[0:1, 1:2], in_=mxc[0:1, 1:2]),
         reads=[r_mxc], writes=[r_mxc])
    si = next_slot()
    P.mm_group([lambda t, si=si: t.matmul(
        slot_ap(si, 0, 128, 0, 1), ones_f[0:1, 0:128], mxc[0:1, 1:2], start=True, stop=True)],
        reads=[r_mxc, r_ones], writes=[slotR[si]])
    P.do("vector", lambda v, si=si: v.tensor_copy(out=mxc[:, 3:4], in_=slot_ap(si, 0, 128, 0, 1)),
         reads=[slotR[si]], writes=[r_mxc])
    P.do("vector", lambda v: v.tensor_scalar(out=w2c[:], in0=w2c[:], scalar1=mxc[:, 3:4],
                                             scalar2=None, op0=ALU.mult),
         reads=[r_mxc], writes=[r_w2c])
    matvec_row([r_w2c] + xt, lambda k: w2c[:, k:k + 1],
               lambda k, h: x_tm[:, k * 1024 + h * 512:k * 1024 + (h + 1) * 512],
               [], 0, nk=NT)
    row_to_cols_bf(0)
    load_T_bf(2 * C)
    matvec_row([r_colsbf], lambda k: cols_bf[:, k:k + 1],
               lambda k, h: qkv_buf[:, k * 1024 + h * 512:k * 1024 + (h + 1) * 512],
               r_qkv, 1)
    broadcast_row_to_bc(1)

    for j in range(NT):
        xa = x_tm[:, j * 1024:(j + 1) * 1024]
        P.do("vector", lambda v, xa=xa, j=j: v.scalar_tensor_tensor(
            out=xa, in0=xa, scalar=1.0, in1=bc_sb[:], op0=ALU.mult, op1=ALU.add,
            accum_out=sumc[:, j:j + 1]),
            reads=[r_bc], writes=[xt[j], r_sumc[j]])
        P.do("scalar", lambda s, xa=xa, j=j: s.activation(
            out=sqd[:], in_=xa, func=AF.Square, accum_out=sqc[:, j:j + 1]),
            reads=[xt[j]], writes=[r_sqd, r_sqc[j]])
        P.do("vector", lambda v, j=j: v.tensor_scalar(
            out=meanc[:, j:j + 1], in0=sumc[:, j:j + 1], scalar1=1.0 / C, scalar2=None,
            op0=ALU.mult), reads=[r_sumc[j]], writes=[r_meanc[j]])
        P.do("vector", lambda v, j=j: v.tensor_tensor(
            out=tmpc[:, j:j + 1], in0=meanc[:, j:j + 1], in1=meanc[:, j:j + 1],
            op=ALU.mult), reads=[r_meanc[j]], writes=[r_tmpc[j]])
        P.do("vector", lambda v, j=j: v.scalar_tensor_tensor(
            out=varc[:, j:j + 1], in0=sqc[:, j:j + 1], scalar=1.0 / C,
            in1=tmpc[:, j:j + 1], op0=ALU.mult, op1=ALU.subtract),
            reads=[r_sqc[j], r_tmpc[j]], writes=[r_varc[j]])
        P.do("scalar", lambda s, j=j: s.activation(
            out=stdc[:, j:j + 1], in_=varc[:, j:j + 1], func=AF.Sqrt, bias=epsc[:]),
            reads=[r_varc[j], r_eps], writes=[r_stdc[j]])
        P.do("vector", lambda v, j=j: v.reciprocal(
            out=rstdc[:, j:j + 1], in_=stdc[:, j:j + 1]),
            reads=[r_stdc[j]], writes=[r_rstdc[j]])
        P.do("vector", lambda v, xa=xa, j=j: v.tensor_scalar(
            out=xa, in0=xa, scalar1=meanc[:, j:j + 1], scalar2=rstdc[:, j:j + 1],
            op0=ALU.subtract, op1=ALU.mult),
            reads=[r_meanc[j], r_rstdc[j]], writes=[xt[j]])
        st = j % 2
        P.do("scalar", lambda s, xa=xa, st=st: s.copy(
            out=y1b_st[:, st * 1024:(st + 1) * 1024], in_=xa),
            reads=[xt[j]], writes=[r_y1bst[st]])
        P.do("scalar", lambda s, j=j, st=st: s.dma_start(
            out=y1b_scr[j * 128:(j + 1) * 128, :],
            in_=y1b_st[:, st * 1024:(st + 1) * 1024]),
            reads=[r_y1bst[st]], writes=[r_y1bscr[j]], dma=True)
        P.do("scalar", lambda s, xa=xa, j=j: s.dma_start(
            out=y1f_scr[j * 128:(j + 1) * 128, :], in_=xa),
            reads=[xt[j]], writes=[r_y1fscr[j]], dma=True)

    if upto == "y1":
        for j in range(NT):
            P.do("scalar", lambda s, j=j: s.dma_start(
                out=out_d[j * 128:(j + 1) * 128, :],
                in_=x_tm[:, j * 1024:(j + 1) * 1024]),
                reads=[xt[j]], dma=True)
        final = P.counts["scalar"]
        P._op("scalar", lambda s: s.nop(), [("scalar", final)], sig=False)
        P.emit()
        return nc

    attn_done = []
    for rr in xt + r_qkv + [r_rows[0], r_rows[1], r_y1bst[0], r_y1bst[1]]:
        if rr.w is not None:
            attn_done.append(rr.w)
        attn_done.extend(t for t in rr.readers if t is not None)
    attn_done = sorted(set(attn_done))
    overlay_rs = (r_ln2t + [r_y1r, r_Hu[0], r_Hu[1], r_ge1[0], r_ge1[1], r_zero, r_U,
                            r_onescol, r_offsall, r_postile, r_tmpa, r_tmpb,
                            r_m1t, r_eoff]
                  + r_Hg + r_rtg + r_y1g[0] + r_y1g[1] + r_rmx
                  + r_probs + r_wte + r_maskt + r_addr
                  + r_yetm + r_ysc
                  + [r_Yesb[h][t] for h in range(CT) for t in range(len(TCS))])
    for rr in overlay_rs:
        rr.init_deps = list(attn_done)

    P.do("vector", lambda v: v.memset(zero_t[:], 0.0), writes=[r_zero])
    P.do("vector", lambda v: v.memset(onescol[:], 1.0), writes=[r_onescol])
    P.do("vector", lambda v: v.memset(offsall[:], 0.0), writes=[r_offsall])
    for e in range(E):
        P.do("vector", lambda v, e=e: v.memset(eoff[:, e:e + 1], float(e * CAPP)),
             writes=[r_eoff])
    P.do("gpsimd", lambda g: g.memset(U128[:], 0.0), writes=[r_U])
    P.do("gpsimd", lambda g: g.affine_select(
        out=U128[:], in_=U128[:], compare_op=ALU.is_gt, fill=1.0,
        base=0, pattern=[[-1, 128]], channel_multiplier=1), writes=[r_U])

    ybuf_zero_tok = [[] for _ in range(E)]
    for e in range(E):
        for k in range(ST):
            t = P.do("sync", lambda s, e=e, k=k: s.dma_start(
                out=ybuf_scr[e * CAPP + k * 128:e * CAPP + (k + 1) * 128, :],
                in_=zero_t[:]), reads=[r_zero], dma=True)
            ybuf_zero_tok[e].append(t)
        P.do("gpsimd", lambda g, e=e: g.dma_start(
            out=ye_scr[e * CAPP + CAP:e * CAPP + CAP + 1, :],
            in_=zero_t[0:1, :]), reads=[r_zero], writes=[r_Yescr[e]], dma=True)

    P.do("scalar", lambda s: s.dma_start(out=bc_sb[0:E, :], in_=router_d),
         writes=[r_bc], dma=True)
    for cc in range(CT):
        si = next_slot()
        P.mm_group([lambda t, cc=cc, si=si: t.transpose(
            slot_ap(si, 0, 128, 0, 8), bc_sb[0:E, cc * 128:(cc + 1) * 128],
            ident_f[0:E, 0:E])],
            reads=[r_bc, r_idf], writes=[slotR[si]])
        P.do("scalar", lambda s, cc=cc, si=si: s.copy(
            out=RwT[:, cc * E:(cc + 1) * E], in_=slot_ap(si, 0, 128, 0, 8)),
            reads=[slotR[si]], writes=[r_RwT])

    do_cast(2)
    rtg = Hg
    ybuf_ready = [[] for _ in range(E)]
    for half in range(NH):
        t0 = half * HN
        y1src = [r_y1bscr[half * HT + jj] for jj in range(HT)]
        for cc in range(CT):
            P.do("scalar", lambda s, cc=cc, t0=t0: s.dma_start(
                out=rtg[:, cc * 1024:(cc + 1) * 1024],
                in_=y1b_scr[t0:t0 + HN, cc * 128:(cc + 1) * 128], transpose=True),
                reads=y1src, writes=[r_rtg[cc]], dma=True)
        for jj in range(HT):
            j = half * HT + jj
            m0 = rmx[:, 2 * j:2 * j + 1]
            m1 = rmx[:, 2 * j + 1:2 * j + 2]
            si = next_slot()
            emits = [lambda t, cc=cc, jj=jj, si=si: t.matmul(
                slot_ap(si, 0, 128, 0, 8),
                rtg[:, cc * 1024 + jj * 128:cc * 1024 + (jj + 1) * 128],
                RwT[:, cc * E:(cc + 1) * E],
                start=(cc == 0), stop=(cc == CT - 1)) for cc in range(CT)]
            P.mm_group(emits, reads=r_rtg + [r_RwT], writes=[slotR[si]])
            pr = probs[:, jj * E:(jj + 1) * E]
            rp = r_probs[jj]
            P.do("vector", lambda v, si=si, m0=m0: v.reduce_max(
                out=m0, in_=slot_ap(si, 0, 128, 0, 8), axis=AX.X, negate=True),
                reads=[slotR[si]], writes=[r_rmx[j]])
            P.do("scalar", lambda s, pr=pr, si=si, m0=m0, m1=m1: s.activation(
                out=pr, in_=slot_ap(si, 0, 128, 0, 8), func=AF.Exp, bias=m0,
                accum_out=m1),
                reads=[slotR[si], r_rmx[j]], writes=[rp, r_rmx[j]])
            P.do("vector", lambda v, m1=m1: v.reciprocal(out=m1, in_=m1),
                 reads=[], writes=[r_rmx[j]])
            P.do("vector", lambda v, pr=pr, m1=m1: v.tensor_scalar(
                out=pr, in0=pr, scalar1=m1, scalar2=None, op0=ALU.mult),
                reads=[r_rmx[j]], writes=[rp])
            ge = ge1[:, 0:E] if j % 2 == 0 else ge1[:, E:2 * E]
            rge = r_ge1[j % 2]
            P.do("vector", lambda v, pr=pr, m0=m0: v.reduce_max(
                out=m0, in_=pr, axis=AX.X),
                reads=[rp], writes=[r_rmx[j]])
            P.do("vector", lambda v, pr=pr, ge=ge, m0=m0: v.tensor_scalar(
                out=ge, in0=pr, scalar1=m0, scalar2=None, op0=ALU.is_ge),
                reads=[rp, r_rmx[j]], writes=[rge])
            P.do("vector", lambda v, ge=ge: v.tensor_copy(out=m1t[:], in_=ge),
                reads=[rge], writes=[r_m1t])
            P.do("vector", lambda v, pr=pr, ge=ge: v.scalar_tensor_tensor(
                out=ge, in0=ge, scalar=-1e9, in1=pr, op0=ALU.mult, op1=ALU.add),
                reads=[rp], writes=[rge])
            P.do("vector", lambda v, ge=ge, m0=m0: v.reduce_max(
                out=m0, in_=ge, axis=AX.X),
                reads=[rge], writes=[r_rmx[j]])
            P.do("vector", lambda v, pr=pr, ge=ge, m0=m0: v.tensor_scalar(
                out=ge, in0=pr, scalar1=m0, scalar2=None, op0=ALU.is_ge),
                reads=[rp, r_rmx[j]], writes=[rge])
            P.do("vector", lambda v, pr=pr, ge=ge, j=j: v.tensor_tensor(
                out=wte[:, j * E:(j + 1) * E], in0=pr, in1=ge, op=ALU.mult),
                reads=[rp, rge], writes=[r_wte[j]])
            P.do("vector", lambda v, j=j: v.tensor_single_scalar(
                out=maskt[:, j * E:(j + 1) * E], in_=wte[:, j * E:(j + 1) * E],
                scalar=0.0, op=ALU.is_gt),
                reads=[r_wte[j]], writes=[r_maskt[j]])
            sp = next_slot()
            P.mm_group([lambda t, j=j, sp=sp: t.matmul(
                slot_ap(sp, 0, 128, 0, 8), U128[:], maskt[:, j * E:(j + 1) * E],
                start=True, stop=True)],
                reads=[r_U, r_maskt[j]], writes=[slotR[sp]])
            sc = next_slot()
            P.mm_group([lambda t, j=j, sc=sc: t.matmul(
                slot_ap(sc, 0, 1, 0, 8), onescol[:], maskt[:, j * E:(j + 1) * E],
                start=True, stop=True)],
                reads=[r_onescol, r_maskt[j]], writes=[slotR[sc]])
            P.do("vector", lambda v, j=j: v.tensor_copy(
                out=offsall[0:1, j * E:(j + 1) * E], in_=wrow[0:1, 0:E]),
                reads=[r_wrow], writes=[r_offsall])
            P.do("vector", lambda v, sc=sc: v.tensor_tensor(
                out=wrow[0:1, 0:E], in0=slot_ap(sc, 0, 1, 0, 8), in1=wrow[0:1, 0:E],
                op=ALU.add), reads=[slotR[sc]], writes=[r_wrow])
            sb = next_slot()
            P.mm_group([lambda t, j=j, sb=sb: t.matmul(
                slot_ap(sb, 0, 128, 0, 8), ones_f[0:1, 0:128],
                offsall[0:1, j * E:(j + 1) * E], start=True, stop=True)],
                reads=[r_ones, r_offsall], writes=[slotR[sb]])
            P.do("vector", lambda v, sp=sp: v.tensor_copy(
                out=postile[:], in_=slot_ap(sp, 0, 128, 0, 8)),
                reads=[slotR[sp]], writes=[r_postile])
            P.do("vector", lambda v, sb=sb: v.scalar_tensor_tensor(
                out=tmpa[:], in0=slot_ap(sb, 0, 128, 0, 8), scalar=-1.0, in1=postile[:],
                op0=ALU.add, op1=ALU.add),
                reads=[slotR[sb], r_postile], writes=[r_tmpa])
            P.do("vector", lambda v: v.tensor_scalar(
                out=tmpa[:], in0=tmpa[:], scalar1=float(CAP), scalar2=None, op0=ALU.min),
                reads=[], writes=[r_tmpa])
            P.do("vector", lambda v: v.tensor_tensor(
                out=tmpb[:], in0=tmpa[:], in1=eoff[:], op=ALU.add),
                reads=[r_tmpa, r_eoff], writes=[r_tmpb])
            P.do("vector", lambda v: v.scalar_tensor_tensor(
                out=postile[:], in0=tmpb[:], scalar=1.0, in1=m1t[:],
                op0=ALU.mult, op1=ALU.mult, accum_out=rmx[:, 0:1]),
                reads=[r_tmpb, r_m1t], writes=[r_postile, r_rmx[0]])
            P.do("vector", lambda v, j=j: v.tensor_tensor(
                out=m1t[:], in0=maskt[:, j * E:(j + 1) * E], in1=m1t[:],
                op=ALU.subtract),
                reads=[r_maskt[j]], writes=[r_m1t])
            P.do("vector", lambda v: v.scalar_tensor_tensor(
                out=postile[:], in0=tmpb[:], scalar=1.0, in1=m1t[:],
                op0=ALU.mult, op1=ALU.mult, accum_out=rmx[:, 1:2]),
                reads=[r_tmpb, r_m1t], writes=[r_postile, r_rmx[0]])
            P.do("vector", lambda v, j=j: v.tensor_copy(
                out=addri[:, 2 * j:2 * j + 2], in_=rmx[:, 0:2]),
                reads=[r_rmx[0]], writes=[r_addr[j]])
        allz = [t for zz in ybuf_zero_tok for t in zz]
        for jj in range(HT):
            j = half * HT + jj
            jr = j % 2
            P.do("scalar", lambda s, j=j, jr=jr: s.dma_start(
                out=ysc[:, jr * 1024:(jr + 1) * 1024],
                in_=y1b_scr[j * 128:(j + 1) * 128, :]),
                reads=[r_y1bscr[j]], writes=[r_ysc[jr]], extra=allz, dma=True)
            for k in range(2):
                t = P.do("gpsimd", lambda g, k=k, j=j, jr=jr: g.indirect_dma_start(
                    out=ybuf_scr,
                    out_offset=IndirectOffsetOnAxis(addri[:, 2 * j + k:2 * j + k + 1], 0),
                    in_=ysc[:, jr * 1024:(jr + 1) * 1024],
                    in_offset=None, element_offset=0),
                    reads=[r_addr[j], r_ysc[jr]], dma=True)
                for e in range(E):
                    ybuf_ready[e].append(t)
    scat_done = sorted(set(t for tt in ybuf_ready for t in tt))
    for rr in r_yetm:
        rr.init_deps.extend(scat_done)
    rt_done = []
    for rr in r_rtg:
        if rr.w is not None:
            rt_done.append(rr.w)
        rt_done.extend(t for t in rr.readers if t is not None)
    for rr in r_Hg:
        rr.init_deps.extend(sorted(set(rt_done)))

    wbuf_idx = 0
    for e in range(E):
        yb = e % 2
        y0 = yb * CT * CAP
        ry = r_y1g[yb]
        if e + 3 < E:
            do_cast(e + 3)
        gb = wbuf_idx % 2; wbuf_idx += 1
        ub = wbuf_idx % 2; wbuf_idx += 1
        db = wbuf_idx % 2; wbuf_idx += 1
        Wg, Wu, Wd = (W1 if gb else W0), (W1 if ub else W0), (W1 if db else W0)
        rWg, rWu, rWd = r_W[gb], r_W[ub], r_W[db]

        for cc in range(CT):
            P.do("sync", lambda s, cc=cc, e=e, y0=y0: s.dma_start(
                out=y1g[:, y0 + cc * CAP:y0 + (cc + 1) * CAP],
                in_=ybuf_scr[e * CAPP:e * CAPP + CAP, cc * 128:(cc + 1) * 128],
                transpose=True),
                reads=[], writes=[ry[cc]], extra=ybuf_ready[e], dma=True)
        if e != 0:
            load_gateup(e, Wg, rWg, gate_scr, r_gscr[e])
            load_gateup(e, Wu, rWu, up_scr, r_uscr[e])

        for s_ in range(ST):
            m0 = rmx[:, 2 * s_:2 * s_ + 1]
            m1 = rmx[:, 2 * s_ + 1:2 * s_ + 2]
            si = next_slot()
            emits = [lambda t, cc=cc, s_=s_, si=si, y0=y0: t.matmul(
                slot_ap(si, 0, 128, 0, 8),
                y1g[:, y0 + cc * CAP + s_ * 128:y0 + cc * CAP + (s_ + 1) * 128],
                RwT[:, cc * E:(cc + 1) * E],
                start=(cc == 0), stop=(cc == CT - 1)) for cc in range(CT)]
            P.mm_group(emits, reads=ry + [r_RwT], writes=[slotR[si]])
            pr = probs[:, s_ * E:(s_ + 1) * E]
            rp = r_probs[s_]
            P.do("vector", lambda v, si=si, m0=m0: v.reduce_max(
                out=m0, in_=slot_ap(si, 0, 128, 0, 8), axis=AX.X, negate=True),
                reads=[slotR[si]], writes=[r_rmx[s_]])
            P.do("scalar", lambda s, pr=pr, si=si, m0=m0, m1=m1: s.activation(
                out=pr, in_=slot_ap(si, 0, 128, 0, 8), func=AF.Exp, bias=m0,
                accum_out=m1),
                reads=[slotR[si], r_rmx[s_]], writes=[rp, r_rmx[s_]])
            P.do("vector", lambda v, m1=m1: v.reciprocal(out=m1, in_=m1),
                 reads=[], writes=[r_rmx[s_]])
            P.do("vector", lambda v, pr=pr, m1=m1: v.tensor_scalar(
                out=pr, in0=pr, scalar1=m1, scalar2=None, op0=ALU.mult),
                reads=[r_rmx[s_]], writes=[rp])
            s1i = next_slot()
            P.mm_group([lambda t, s_=s_, e=e, s1i=s1i: t.matmul(
                slot_ap(s1i, 0, 1, 0, 128), probs[:, s_ * E + e:s_ * E + e + 1],
                ident_f[:], start=True, stop=True)],
                reads=[rp, r_idf], writes=[slotR[s1i]])
            P.do("vector", lambda v, s1i=s1i: v.tensor_copy(
                out=wrow[:], in_=slot_ap(s1i, 0, 1, 0, 128)),
                reads=[slotR[s1i]], writes=[r_wrow])
            s2i = next_slot()
            P.mm_group([lambda t, s2i=s2i: t.matmul(
                slot_ap(s2i, 0, 128, 0, 128), ones_f[0:1, 0:128],
                wrow[:], start=True, stop=True)],
                reads=[r_wrow, r_ones], writes=[slotR[s2i]])
            P.do("vector", lambda v, s_=s_, s2i=s2i: v.tensor_copy(
                out=bc_sb[:, s_ * 128:(s_ + 1) * 128],
                in_=slot_ap(s2i, 0, 128, 0, 128)),
                reads=[slotR[s2i]], writes=[r_bc])

        for ii in range(IT):
            pi = ii % 2
            prA, prB = pairR(pi)
            emits = []
            pre = []
            for (tco, tcl) in TCS:
                for cc in range(CT):
                    emits.append(lambda t, ii=ii, tco=tco, tcl=tcl, cc=cc, pi=pi, Wg=Wg, y0=y0: t.matmul(
                        pair_ap(pi, tco, tco + tcl),
                        Wg[:, cc * I + ii * 128:cc * I + (ii + 1) * 128],
                        y1g[:, y0 + cc * CAP + tco:y0 + cc * CAP + tco + tcl],
                        start=(cc == 0), stop=(cc == CT - 1)))
                    pre.append([rWg[cc], ry[cc]])
            P.mm_group(emits, writes=[prA, prB], per_reads=pre, reg_reads=rWg[CT:])
            P.do("scalar", lambda s, ii=ii, pi=pi: s.activation(
                out=Hg[:, ii * CAP:(ii + 1) * CAP], in_=pair_ap(pi, 0, CAP),
                func=AF.Silu),
                reads=[prA, prB], writes=[r_Hg[ii]])

        for ii in range(IT):
            P.do("scalar", lambda s, ii=ii, e=e, Wd=Wd: s.dma_start(
                out=Wd[:, ii * 1024:(ii + 1) * 1024],
                in_=down_scr[e * C:(e + 1) * C, ii * 128:(ii + 1) * 128],
                transpose=True),
                reads=[r_dscr[e]], writes=[rWd[ii]], dma=True)

        for ii in range(IT):
            pi = 2 + ii % 2
            prA, prB = pairR(pi)
            emits = []
            pre = []
            for (tco, tcl) in TCS:
                for cc in range(CT):
                    emits.append(lambda t, ii=ii, tco=tco, tcl=tcl, cc=cc, pi=pi, Wu=Wu, y0=y0: t.matmul(
                        pair_ap(pi, tco, tco + tcl),
                        Wu[:, cc * I + ii * 128:cc * I + (ii + 1) * 128],
                        y1g[:, y0 + cc * CAP + tco:y0 + cc * CAP + tco + tcl],
                        start=(cc == 0), stop=(cc == CT - 1)))
                    pre.append([rWu[cc], ry[cc]])
            P.mm_group(emits, writes=[prA, prB], per_reads=pre, reg_reads=rWu[CT:])
            hs = ii % 2
            P.do("vector", lambda v, hs=hs, pi=pi: v.tensor_tensor(
                out=Hu[:, hs * CAP:(hs + 1) * CAP], in0=pair_ap(pi, 0, CAP),
                in1=bc_sb[:, 0:CAP], op=ALU.mult),
                reads=[prA, prB, r_bc], writes=[r_Hu[hs]])
            heng = "gpsimd" if ii % 2 == 0 else "vector"
            P.do(heng, lambda g, ii=ii, hs=hs: g.tensor_tensor(
                out=Hg[:, ii * CAP:(ii + 1) * CAP],
                in0=Hg[:, ii * CAP:(ii + 1) * CAP],
                in1=Hu[:, hs * CAP:(hs + 1) * CAP], op=ALU.mult),
                reads=[r_Hu[hs]], writes=[r_Hg[ii]])

        for h in range(CT):
            for tci, (tco, tcl) in enumerate(TCS):
                si = next_slot()
                emits = [lambda t, ii=ii, h=h, tco=tco, tcl=tcl, si=si, Wd=Wd: t.matmul(
                    slot_ap(si, 0, 128, 0, tcl),
                    Wd[:, ii * 1024 + h * 128:ii * 1024 + (h + 1) * 128],
                    Hg[:, ii * CAP + tco:ii * CAP + tco + tcl],
                    start=(ii == 0), stop=(ii == IT - 1)) for ii in range(IT)]
                pre = [[rWd[ii], r_Hg[ii]] for ii in range(IT)]
                P.mm_group(emits, writes=[slotR[si]], per_reads=pre)
                P.do("vector", lambda v, h=h, tco=tco, tcl=tcl, si=si: v.tensor_copy(
                    out=Ye_sb[:, h * CAP + tco:h * CAP + tco + tcl],
                    in_=slot_ap(si, 0, 128, 0, tcl)),
                    reads=[slotR[si]], writes=[r_Yesb[h][tci]])

        for s_ in range(ST):
            sr = s_ % 2
            yreads = [r_Yesb[ccf][0 if s_ * 128 < 512 else 1] for ccf in range(CT)]
            for ccf in range(CT):
                si = next_slot()
                P.mm_group([lambda t, ccf=ccf, s_=s_, si=si: t.transpose(
                    slot_ap(si, 0, 128, 0, 128),
                    Ye_sb[:, ccf * CAP + s_ * 128:ccf * CAP + (s_ + 1) * 128],
                    ident_f[:])],
                    reads=[yreads[ccf], r_idf], writes=[slotR[si]])
                P.do("scalar", lambda s, ccf=ccf, sr=sr, si=si: s.copy(
                    out=yetm[:, sr * 1024 + ccf * 128:sr * 1024 + (ccf + 1) * 128],
                    in_=slot_ap(si, 0, 128, 0, 128)),
                    reads=[slotR[si]], writes=[r_yetm[sr]])
            P.do("scalar", lambda s, e=e, s_=s_, sr=sr: s.dma_start(
                out=ye_scr[e * CAPP + s_ * 128:e * CAPP + (s_ + 1) * 128, :],
                in_=yetm[:, sr * 1024:(sr + 1) * 1024]),
                reads=[r_yetm[sr]], writes=[r_Yescr[e]], dma=True)

    for j in range(NT):
        sl = j % 2
        lt = ln2t[:, sl * 1024:(sl + 1) * 1024]
        rlt = r_ln2t[sl]
        for k in range(2):
            P.do("gpsimd", lambda g, k=k, j=j, lt=lt: g.indirect_dma_start(
                out=lt, out_offset=None,
                in_=ye_scr,
                in_offset=IndirectOffsetOnAxis(addri[:, 2 * j + k:2 * j + k + 1], 0),
                compute_op=(ALU.bypass if k == 0 else ALU.add)),
                reads=[r_addr[j]] + r_Yescr, writes=[rlt], dma=True)
        P.do("scalar", lambda s, j=j: s.dma_start(
            out=y1r[:], in_=y1f_scr[j * 128:(j + 1) * 128, :]),
            reads=[r_y1fscr[j]], writes=[r_y1r], dma=True)
        P.do("vector", lambda v, lt=lt: v.tensor_tensor(
            out=lt, in0=lt, in1=y1r[:], op=ALU.add),
            reads=[r_y1r], writes=[rlt])
        P.do("vector", lambda v, j=j, lt=lt: v.reduce_sum(
            out=sumc[:, j:j + 1], in_=lt, axis=AX.X),
            reads=[rlt], writes=[r_sumc[j]])
        P.do("scalar", lambda s, j=j, lt=lt: s.activation(
            out=sqd[:], in_=lt, func=AF.Square, accum_out=sqc[:, j:j + 1]),
            reads=[rlt], writes=[r_sqd, r_sqc[j]])
        P.do("vector", lambda v, j=j: v.tensor_scalar(
            out=meanc[:, j:j + 1], in0=sumc[:, j:j + 1], scalar1=1.0 / C,
            scalar2=None, op0=ALU.mult), reads=[r_sumc[j]], writes=[r_meanc[j]])
        P.do("vector", lambda v, j=j: v.tensor_tensor(
            out=tmpc[:, j:j + 1], in0=meanc[:, j:j + 1], in1=meanc[:, j:j + 1],
            op=ALU.mult), reads=[r_meanc[j]], writes=[r_tmpc[j]])
        P.do("vector", lambda v, j=j: v.scalar_tensor_tensor(
            out=varc[:, j:j + 1], in0=sqc[:, j:j + 1], scalar=1.0 / C,
            in1=tmpc[:, j:j + 1], op0=ALU.mult, op1=ALU.subtract),
            reads=[r_sqc[j], r_tmpc[j]], writes=[r_varc[j]])
        P.do("scalar", lambda s, j=j: s.activation(
            out=stdc[:, j:j + 1], in_=varc[:, j:j + 1], func=AF.Sqrt, bias=epsc[:]),
            reads=[r_varc[j], r_eps], writes=[r_stdc[j]])
        P.do("vector", lambda v, j=j: v.reciprocal(
            out=rstdc[:, j:j + 1], in_=stdc[:, j:j + 1]),
            reads=[r_stdc[j]], writes=[r_rstdc[j]])
        P.do("vector", lambda v, j=j, lt=lt: v.tensor_scalar(
            out=lt, in0=lt, scalar1=meanc[:, j:j + 1],
            scalar2=rstdc[:, j:j + 1], op0=ALU.subtract, op1=ALU.mult),
            reads=[r_meanc[j], r_rstdc[j]], writes=[rlt])
        P.do("scalar", lambda s, j=j, lt=lt: s.dma_start(
            out=out_d[j * 128:(j + 1) * 128, :], in_=lt),
            reads=[rlt], dma=True)

    final = P.counts["scalar"]
    P._op("scalar", lambda s: s.nop(), [("scalar", final)], sig=False)
    P.emit()
    return nc


def kernel(L0_attn_x, L0_attn_w_qkv, L0_attn_w_conv1, L0_ffn_router_w,
           L0_ffn_gate_w, L0_ffn_up_w, L0_ffn_down_w, _upto="full"):
    x = np.ascontiguousarray(np.asarray(L0_attn_x, dtype=np.float32))
    wqkv = np.ascontiguousarray(np.asarray(L0_attn_w_qkv, dtype=np.float32))
    wconv = np.ascontiguousarray(np.asarray(L0_attn_w_conv1, dtype=np.float32)).reshape(1, C)
    router = np.ascontiguousarray(np.asarray(L0_ffn_router_w, dtype=np.float32))
    gate = np.ascontiguousarray(np.asarray(L0_ffn_gate_w, dtype=np.float32)).reshape(E * I, C)
    up = np.ascontiguousarray(np.asarray(L0_ffn_up_w, dtype=np.float32)).reshape(E * I, C)
    down = np.ascontiguousarray(np.asarray(L0_ffn_down_w, dtype=np.float32)).reshape(E * C, I)

    nc = build(_upto)
    in_maps = []
    for b in range(8):
        in_maps.append({
            "x": np.ascontiguousarray(x[b]),
            "wqkv": wqkv, "wconv": wconv, "router": router,
            "gate": gate, "up": up, "down": down,
        })
    res = run_bass_kernel_spmd(nc, in_maps, core_ids=list(range(8)))
    return np.stack([res.results[b]["out"] for b in range(8)], axis=0)
